# version 13
# baseline (speedup 1.0000x reference)
import numpy as np
import ml_dtypes
import concourse.bass as bass
import concourse.mybir as mybir
import concourse.bass_utils as _bass_utils
from concourse.bass_utils import run_bass_kernel_spmd

if not getattr(_bass_utils, "_ldw_opt_patched", False):
    _orig_run_command = _bass_utils.run_command

    def _run_command_ldw(argv, **kwargs):
        argv = ["--enable-ldw-opt=true" if a == "--enable-ldw-opt=false" else a
                for a in argv]
        return _orig_run_command(argv, **kwargs)

    _bass_utils.run_command = _run_command_ldw
    _bass_utils._ldw_opt_patched = True

F32 = mybir.dt.float32
BF16 = mybir.dt.bfloat16
U8 = mybir.dt.uint8
AF = mybir.ActivationFunctionType
ALU = mybir.AluOpType
AX = mybir.AxisListType.X

T = 2048
C = 1024
FC = 8
H = 16
DH = 64
NPAIR = 8
DFF = 4096
EPS = 1e-5

N_SUB = 6



def _pair_cols(p):
    h0, h1 = 2 * p, 2 * p + 1
    q0 = np.arange(h0 * DH, (h0 + 1) * DH)
    q1 = np.arange(h1 * DH, (h1 + 1) * DH)
    return q0, q1


def _prep(inputs):
    d = {}
    f4 = np.float32

    X = np.asarray(inputs["L0_attn_X"], f4)
    xcores = [np.ascontiguousarray(X[i].T).reshape(FC, 128, T) for i in range(8)]

    Win = np.asarray(inputs["L0_attn_W_in"], f4)
    bin_ = np.asarray(inputs["L0_attn_b_in"], f4)
    wq0 = np.empty((NPAIR, FC, 128, 384), f4)
    bq0 = np.empty((128, 3, NPAIR), f4)
    for p in range(NPAIR):
        q0, q1 = _pair_cols(p)
        cols = np.concatenate([q0, q1, C + q0, C + q1, 2 * C + q0, 2 * C + q1])
        wq0[p] = Win[:, cols].reshape(FC, 128, 384)
        bq0[:, 0, p] = bin_[cols[0:128]]
        bq0[:, 1, p] = bin_[cols[128:256]]
        bq0[:, 2, p] = bin_[cols[256:384]]
    d["wqkv0"] = wq0
    d["bqkv0"] = np.ascontiguousarray(bq0)

    Wout = np.asarray(inputs["L0_attn_W_out"], f4)
    d["wout0"] = np.ascontiguousarray(
        Wout.reshape(FC, 128, FC, 128).transpose(2, 0, 1, 3))
    d["bout0"] = np.ascontiguousarray(
        np.asarray(inputs["L0_attn_b_out"], f4).reshape(FC, 128).T)
    gb = np.empty((128, 16), f4)
    gb[:, 0:8] = np.asarray(inputs["L0_attn_gamma"], f4).reshape(FC, 128).T
    gb[:, 8:16] = np.asarray(inputs["L0_attn_beta"], f4).reshape(FC, 128).T
    d["gb0"] = np.ascontiguousarray(gb)

    def t_lay(W):
        return np.ascontiguousarray(
            W.reshape(FC, 128, FC, 128).transpose(0, 2, 3, 1))
    d["wr0"] = t_lay(np.asarray(inputs["L0_ffn_W_r"], f4))
    d["wk0"] = t_lay(np.asarray(inputs["L0_ffn_W_k"], f4))
    d["wv0"] = t_lay(np.asarray(inputs["L0_ffn_W_v"], f4))
    mus = np.empty((128, 16), f4)
    mus[:, 0:8] = np.asarray(inputs["L0_ffn_mu_r"], f4).reshape(FC, 128).T
    mus[:, 8:16] = np.asarray(inputs["L0_ffn_mu_k"], f4).reshape(FC, 128).T
    d["mus0"] = np.ascontiguousarray(mus)

    WqT = np.asarray(inputs["L1_attn_Wq"], f4).T
    WkT = np.asarray(inputs["L1_attn_Wk"], f4).T
    WvT = np.asarray(inputs["L1_attn_Wv"], f4).T
    bq = np.asarray(inputs["L1_attn_bq"], f4)
    bk = np.asarray(inputs["L1_attn_bk"], f4)
    bv = np.asarray(inputs["L1_attn_bv"], f4)
    wq1 = np.empty((NPAIR, FC, 128, 384), f4)
    bq1 = np.empty((128, 3, NPAIR), f4)
    for p in range(NPAIR):
        q0, q1 = _pair_cols(p)
        cols = np.concatenate([q0, q1])
        wq1[p, :, :, 0:128] = WqT[:, cols].reshape(FC, 128, 128)
        wq1[p, :, :, 128:256] = WkT[:, cols].reshape(FC, 128, 128)
        wq1[p, :, :, 256:384] = WvT[:, cols].reshape(FC, 128, 128)
        bq1[:, 0, p] = bq[cols]
        bq1[:, 1, p] = bk[cols]
        bq1[:, 2, p] = bv[cols]
    d["wqkv1"] = wq1
    d["bqkv1"] = np.ascontiguousarray(bq1)
    WoT = np.asarray(inputs["L1_attn_Wo"], f4).T
    d["wo1"] = np.ascontiguousarray(
        WoT.reshape(FC, 128, FC, 128).transpose(2, 0, 1, 3))
    d["bo1"] = np.ascontiguousarray(
        np.asarray(inputs["L1_attn_bo"], f4).reshape(FC, 128).T)

    W1 = np.asarray(inputs["L1_ffn_W1"], f4)
    d["w1t"] = np.ascontiguousarray(
        W1.reshape(32, 128, FC, 128).transpose(0, 2, 3, 1))
    W2 = np.asarray(inputs["L1_ffn_W2"], f4)
    d["w2t"] = np.ascontiguousarray(
        W2.reshape(FC, 128, 32, 128).transpose(0, 2, 3, 1))
    d["b1"] = np.ascontiguousarray(
        np.asarray(inputs["L1_ffn_b1"], f4).reshape(32, 128).T)
    d["b2"] = np.ascontiguousarray(
        np.asarray(inputs["L1_ffn_b2"], f4).reshape(FC, 128).T)

    bf = ml_dtypes.bfloat16
    d["onesb"] = np.ones((128, 128), bf)
    d["identb"] = np.eye(128, dtype=bf)
    d["identf"] = np.eye(128, dtype=f4)
    rm = np.zeros((128, 128), np.float64)
    for base in (0, 64):
        for i in range(16):
            rm[base + 2 * i + 1, base + 2 * i] = -1.0
            rm[base + 2 * i, base + 2 * i + 1] = 1.0
    d["rmatb"] = rm.astype(bf)
    cosr, sinr = _rope_tables_f32()
    ct = np.ones((128, T), f4)
    st = np.zeros((128, T), f4)
    for base in (0, 64):
        ct[base:base + 32] = cosr.T
        st[base:base + 32] = sinr.T
    d["ctab"] = np.ascontiguousarray(ct)
    d["stab"] = np.ascontiguousarray(st)
    return d, xcores


def _rope_tables_f32():
    f4 = np.float32
    invf = (1.0 / 10000.0 ** (np.arange(0, 32, 2, dtype=f4) / f4(32))).astype(f4)
    tn = np.arange(T, dtype=f4)
    freqs = np.repeat((tn[:, None] * invf[None, :]).astype(f4), 2, axis=-1)
    try:
        import jax
        import jax.numpy as jnp
        cpu = jax.devices("cpu")[0]
        with jax.default_device(cpu):
            c = np.asarray(jnp.cos(jnp.asarray(freqs)))
            s = np.asarray(jnp.sin(jnp.asarray(freqs)))
        return c.astype(f4), s.astype(f4)
    except Exception:
        f8 = np.float64
        return (np.cos(freqs.astype(f8)).astype(f4),
                np.sin(freqs.astype(f8)).astype(f4))




class Sem:
    def __init__(self, name):
        self.name, self.val, self.h = name, 0, None


class Emit:
    ENGS = ("tensor", "vector", "scalar", "sync", "gpsimd")

    def __init__(self):
        self.stream = {e: [] for e in self.ENGS}
        self.sems = {n: Sem(n) for n in (
            "SD", "GD", "TE", "VE", "SE", "GE", "BR",
            "WQ", "WA0", "WA1", "WB0", "WB1", "W20", "W21", "PP", "PP2",
            "PQ0", "PQ1")}

    AUTO = {"vector": "VE", "scalar": "SE", "gpsimd": "GE"}

    def op(self, eng, fn, waits=(), incs=()):
        waits = list(waits)
        if eng in self.AUTO:
            s = self.sems[self.AUTO[eng]]
            waits.append((s, s.val))
        waits = [(s, v) for (s, v) in waits if v > 0]
        incv = []
        for s, k in incs:
            s.val += k
            incv.append((s, k))

        def run(e):
            for s, v in waits:
                e.wait_ge(s.h, v)
            inst = fn(e)
            if incv:
                if inst is None:
                    inst = e.nop()
                for s, k in incv:
                    inst = inst.then_inc(s.h, k)
        self.stream[eng].append(run)

    def barrier(self, engines=("tensor", "vector", "scalar", "sync")):
        BR = self.sems["BR"]
        target = BR.val + len(engines)
        for e in engines:
            self.op(e, lambda eng: eng.nop(), incs=[(BR, 1)])
        for e in engines:
            self.op(e, lambda eng: None, waits=[(BR, target)])



_CACHE = {}


def build_nc(n_sub):
    nc = bass.Bass("TRN2", target_bir_lowering=False)

    def din(name, shape, dt=F32):
        return nc.dram_tensor(name, list(shape), dt, kind="ExternalInput").ap()

    xin = din("xin", (FC, 128, T))
    wqkv0 = din("wqkv0", (NPAIR, FC, 128, 384))
    bqkv0 = din("bqkv0", (128, 3, NPAIR))
    wout0 = din("wout0", (FC, FC, 128, 128))
    bout0 = din("bout0", (128, FC))
    gb0 = din("gb0", (128, 16))
    wr0 = din("wr0", (FC, FC, 128, 128))
    wk0 = din("wk0", (FC, FC, 128, 128))
    wv0 = din("wv0", (FC, FC, 128, 128))
    mus0 = din("mus0", (128, 16))
    wqkv1 = din("wqkv1", (NPAIR, FC, 128, 384))
    bqkv1 = din("bqkv1", (128, 3, NPAIR))
    wo1 = din("wo1", (FC, FC, 128, 128))
    bo1 = din("bo1", (128, FC))
    w1t = din("w1t", (32, FC, 128, 128))
    b1v = din("b1", (128, 32))
    w2t = din("w2t", (FC, 32, 128, 128))
    b2v = din("b2", (128, FC))
    ctab_d = din("ctab", (128, T))
    stab_d = din("stab", (128, T))
    onesb_d = din("onesb", (128, 128), BF16)
    identb_d = din("identb", (128, 128), BF16)
    identf_d = din("identf", (128, 128), F32)
    rmatb_d = din("rmatb", (128, 128), BF16)

    xout = nc.dram_tensor("xout", [FC, 128, T], F32, kind="ExternalOutput").ap()
    wq1s = nc.dram_tensor("wq1s", [32 * FC, 128, 128], BF16, kind="Internal").ap()
    wq2s = nc.dram_tensor("wq2s", [FC * 32, 128, 128], BF16, kind="Internal").ap()

    TOTAL = 212832
    nc.alloc_sbuf_tensor("arena", [128, TOTAL], U8)
    arena_base = None
    for a in nc.m.functions[0].allocations:
        if not isinstance(a, mybir.MemoryLocationSet):
            continue
        for ml in (a.memorylocations or []):
            if ml.name == "arena":
                arena_base = ml.addr
    assert arena_base is not None

    off = {}
    cur = 0

    def region(name, size):
        nonlocal cur
        off[name] = cur
        cur += size
    region("X", 65536)
    region("REPS", 24576)
    region("RA", 32768)
    region("RB", 32768)
    region("RC", 33536)
    region("RD", 8448)
    region("RS", 4096)
    region("RW", 8192)
    region("CONST", 2560)
    assert cur <= TOTAL, cur

    def at(name, shape, dt, reg, extra=0):
        return nc.alloc_sbuf_tensor_at(name, list(shape), dt,
                                       offset=arena_base + off[reg] + extra)

    x_t = at("x_t", (128, FC, T), F32, "X")
    reps = at("reps", (128, 3, T), F32, "REPS")
    rs_t = at("rs_t", (128, 1024), F32, "RS")

    ones_b = at("ones_b", (128, 128), BF16, "CONST", 0)
    ident_b = at("ident_b", (128, 128), BF16, "CONST", 256)
    rmat_b = at("rmat_b", (128, 128), BF16, "CONST", 512)
    bqkv0_t = at("bqkv0_t", (128, 3, NPAIR), F32, "CONST", 768)
    bqkv1_t = at("bqkv1_t", (128, 3, NPAIR), F32, "CONST", 864)
    bout0_t = at("bout0_t", (128, FC), F32, "CONST", 960)
    bo1_t = at("bo1_t", (128, FC), F32, "CONST", 992)
    gb0_t = at("gb0_t", (128, 16), F32, "CONST", 1024)
    mus0_t = at("mus0_t", (128, 16), F32, "CONST", 1088)
    b1_t = at("b1_t", (128, 32), F32, "CONST", 1152)
    b2_t = at("b2_t", (128, FC), F32, "CONST", 1280)
    thr_t = at("thr_t", (128, 8), F32, "CONST", 1312)
    acc_t = at("acc_t", (128, 8), F32, "CONST", 1344)
    nmx_t = at("nmx_t", (128, 8), F32, "CONST", 1504)
    ones_f32 = at("ones_f32", (128, 128), F32, "CONST", 1536)
    ident_f = at("ident_f", (128, 128), F32, "CONST", 2048)

    xr_t = at("xr_t", (128, FC, T), BF16, "RA")
    xk_t = at("xk_t", (128, FC, T), BF16, "RB")
    kact_t = at("kact_t", (128, FC, T), BF16, "RC")
    xnscr = at("xnscr", (128, T + 1), F32, "RD")

    ctx_all = at("ctx_all", (128, FC, T), BF16, "RA")
    qb_t = at("qb_t", (128, T), BF16, "RB", 0)
    kb_t = at("kb_t", (128, T), BF16, "RB", 4096)
    vstg_t = at("vstg_t", (128, 1024), F32, "RB", 8192)
    vt_t = at("vt_t", (128, 2, 16, 128), BF16, "RB", 12288)
    pt_t = at("pt_t", (128, 2, 1024), BF16, "RB", 20480)
    qsq_t = at("qsq_t", (128, 1024), BF16, "RB", 24576)
    rinv_t = at("rinv_t", (128, 1024), BF16, "RB", 26624)
    cstg_t = at("cstg_t", (64, 1024), BF16, "RB", 28672)
    lnxc_t = at("lnxc_t", (128, 2, T), BF16, "RB", 0)
    lnxs_t = at("lnxs_t", (128, 2, T), BF16, "RB", 16384)
    xn_t = at("xn_t", (128, FC, T + 1), BF16, "RC")
    ctab_t = at("ctab_t", (128, T), F32, "RD", 0)

    hlo_t = at("hlo_t", (128, 16, 1024), BF16, "RA")
    hhi_t = at("hhi_t", (128, 16, 1024), BF16, "RB")
    xnf_t = at("xnf_t", (128, FC, T), BF16, "RC")
    w2buf = at("w2buf", (128, 2, 16, 128), BF16, "RD")
    ppw_t = at("ppw_t", (128, 2, 512), F32, "RD", 0)
    ppb_t = at("ppb_t", (128, 512), BF16, "RD", 4096)
    ppq_t = at("ppq_t", (128, 2, 512), BF16, "RD", 5120)

    wbufA = at("wbufA", (128, 2, FC, 128), BF16, "RW", 0)
    wbufB = at("wbufB", (128, 2, FC, 128), BF16, "RW", 4096)
    wqkv_t = at("wqkv_t", (128, FC, 384), BF16, "RW", 0)
    stab_view = at("stab_view", (128, T), F32, "REPS", 2 * T * 4)

    pa = nc.alloc_psum_tensor("pa", [128, 2048], F32)
    pb = nc.alloc_psum_tensor("pb", [128, 2048], F32)

    E = Emit()
    SD, GD, TE, VE, SE, GE = (E.sems[n] for n in ("SD", "GD", "TE", "VE", "SE", "GE"))
    WQ, WA0, WA1, WB0, WB1, W20, W21, PP, PP2 = (
        E.sems[n] for n in ("WQ", "WA0", "WA1", "WB0", "WB1", "W20", "W21",
                            "PP", "PP2"))
    PPS = (PP, PP2)
    PQS = (E.sems["PQ0"], E.sems["PQ1"])
    WA = (WA0, WA1)
    WB = (WB0, WB1)
    W2S = (W20, W21)

    def dma(eng_name, out_ap, in_ap, sem=None):
        sem = SD if sem is None else sem
        E.op(eng_name, lambda e, o=out_ap, i=in_ap: e.dma_start(o, i),
             incs=[(sem, 16)])

    dma("sync", x_t[:, :, :], xin.rearrange("c p t -> p c t"))
    dma("sync", ones_b[:, :], onesb_d)
    dma("sync", ident_b[:, :], identb_d)
    dma("sync", ident_f[:, :], identf_d)
    dma("sync", rmat_b[:, :], rmatb_d)
    dma("sync", bqkv0_t[:, :, :], bqkv0)
    dma("sync", bqkv1_t[:, :, :], bqkv1)
    dma("sync", bout0_t[:, :], bout0)
    dma("sync", bo1_t[:, :], bo1)
    dma("sync", gb0_t[:, :], gb0)
    dma("sync", mus0_t[:, :], mus0)
    dma("sync", b1_t[:, :], b1v)
    dma("sync", b2_t[:, :], b2v)
    prolog_sd = SD.val

    E.op("vector", lambda e: e.memset(xn_t[:, :, 0:1], 0.0),
         waits=[(SD, prolog_sd)], incs=[(VE, 1)])
    E.op("vector", lambda e: e.memset(ones_f32[:, :], 1.0), incs=[(VE, 1)])


    def emit_ln(scr_xc, scr_xs, want_g=False):
        te0 = TE.val
        te_read = [te0, te0]
        se_sq = [0, 0]
        for c in range(FC):
            b = c % 2
            E.op("vector",
                 lambda e, c=c, b=b: e.tensor_copy(scr_xc[:, b, :], x_t[:, c, :]),
                 waits=[(TE, te_read[b])], incs=[(VE, 1)])
            v_cast = VE.val
            E.op("scalar",
                 lambda e, c=c, b=b: e.activation(scr_xs[:, b, :], scr_xc[:, b, :],
                                                  AF.Square),
                 waits=[(VE, v_cast), (TE, te_read[b])], incs=[(SE, 1)])
            se_sq[b] = SE.val

            def mm_sum(t, c=c, b=b):
                r = None
                for g in range(4):
                    r = t.matmul(pa[:, g * 512:(g + 1) * 512], ones_b[:, :],
                                 scr_xc[:, b, g * 512:(g + 1) * 512],
                                 start=(c == 0), stop=(c == FC - 1))
                return r
            E.op("tensor", mm_sum, waits=[(VE, v_cast)], incs=[(TE, 1)])

            def mm_sq(t, c=c, b=b):
                r = None
                for g in range(4):
                    r = t.matmul(pb[:, g * 512:(g + 1) * 512], ones_b[:, :],
                                 scr_xs[:, b, g * 512:(g + 1) * 512],
                                 start=(c == 0), stop=(c == FC - 1))
                return r
            E.op("tensor", mm_sq, waits=[(SE, se_sq[b])], incs=[(TE, 1)])
            te_read[b] = TE.val
        te_stats = TE.val
        E.op("vector", lambda e: e.tensor_scalar(reps[:, 0, :], pa[:, :], 1.0 / C,
                                                 None, ALU.mult),
             waits=[(TE, te_stats)], incs=[(VE, 1)])
        E.op("vector", lambda e: e.tensor_scalar(reps[:, 2, :], pb[:, :], 1.0 / C,
                                                 EPS, ALU.mult, ALU.add),
             incs=[(VE, 1)])
        E.op("vector", lambda e: e.tensor_mul(reps[:, 1, :], reps[:, 0, :],
                                              reps[:, 0, :]),
             incs=[(VE, 1)])
        E.op("vector", lambda e: e.tensor_sub(reps[:, 2, :], reps[:, 2, :],
                                              reps[:, 1, :]),
             incs=[(VE, 1)])
        v_var = VE.val
        E.op("scalar", lambda e: e.activation(reps[:, 1, :], reps[:, 2, :], AF.Ln),
             waits=[(VE, v_var)], incs=[(SE, 1)])
        if want_g:
            for hf in range(2):
                sl = slice(hf * 1024, (hf + 1) * 1024)
                E.op("scalar", lambda e, sl=sl: e.activation(
                    rs_t[:, :], reps[:, 1, sl], AF.Exp, scale=-1.0),
                     incs=[(SE, 1)])
                s_r = SE.val
                E.op("vector", lambda e: e.tensor_scalar(
                    rs_t[:, :], rs_t[:, :], -1.0, 1.0, ALU.mult, ALU.add),
                     waits=[(SE, s_r)], incs=[(VE, 1)])
                E.op("vector", lambda e: e.tensor_scalar(
                    rs_t[:, :], rs_t[:, :], EPS, 1.0, ALU.mult, ALU.add),
                     incs=[(VE, 1)])
                E.op("vector", lambda e, sl=sl: e.tensor_mul(
                    rs_t[:, :], rs_t[:, :], reps[:, 2, sl]),
                     incs=[(VE, 1)])
                v_b = VE.val
                E.op("scalar", lambda e: e.activation(rs_t[:, :], rs_t[:, :], AF.Ln),
                     waits=[(VE, v_b)], incs=[(SE, 1)])
                E.op("scalar", lambda e, sl=sl: e.activation(
                    reps[:, 2, sl], rs_t[:, :], AF.Exp, scale=-0.5),
                     incs=[(SE, 1)])
        E.op("scalar", lambda e: e.activation(reps[:, 1, :], reps[:, 1, :], AF.Exp,
                                              scale=-0.5),
             incs=[(SE, 1)])
        return SE.val


    def emit_ffn0():
        s_stats = emit_ln(lnxc_t, lnxs_t)
        E.op("vector", lambda e: e.memset(xnscr[:, 0:1], 0.0),
             waits=[(PP, PP.val), (PP2, PP2.val)] + list(pp_spill_guard[0]),
             incs=[(VE, 1)])
        mix_k_done = [0] * FC
        for c in range(FC):
            E.op("vector", lambda e, c=c: e.tensor_sub(
                xnscr[:, 1:T + 1], x_t[:, c, :], reps[:, 0, :]),
                 waits=[(SE, s_stats)], incs=[(VE, 1)])
            E.op("vector", lambda e: e.tensor_mul(
                xnscr[:, 1:T + 1], xnscr[:, 1:T + 1], reps[:, 1, :]),
                 incs=[(VE, 1)])
            for hf in range(2):
                a, b = hf * 1024, (hf + 1) * 1024
                E.op("vector", lambda e, a=a, b=b: e.tensor_sub(
                    rs_t[:, :], xnscr[:, a + 1:b + 1], xnscr[:, a:b]),
                     incs=[(VE, 1)])
                E.op("vector", lambda e, c=c, a=a, b=b: e.scalar_tensor_tensor(
                    xr_t[:, c, a:b], rs_t[:, :], mus0_t[:, c:c + 1], xnscr[:, a:b],
                    ALU.mult, ALU.add),
                     incs=[(VE, 1)])
                E.op("vector", lambda e, c=c, a=a, b=b: e.scalar_tensor_tensor(
                    xk_t[:, c, a:b], rs_t[:, :], mus0_t[:, 8 + c:9 + c],
                    xnscr[:, a:b], ALU.mult, ALU.add),
                     incs=[(VE, 1)])
            mix_k_done[c] = VE.val
        v_mix = VE.val
        pp_entry_ve[0] = v_mix

        def wload(buf, slot, src, wait_te, sems):
            sm = sems[slot]
            E.op("gpsimd", lambda e, buf=buf, slot=slot, src=src: e.dma_start(
                buf[:, slot, :, :], src.rearrange("k p c -> p k c")),
                 waits=[(TE, wait_te)], incs=[(sm, 16)])
            return (sm, sm.val)

        wk_gd = [None] * FC
        te_entry = TE.val
        wk_gd[0] = wload(wbufA, 0, wk0[0], te_entry, WA)
        wk_gd[1] = wload(wbufA, 1, wk0[1], te_entry, WA)
        wk_last_te = [0, 0]
        ve_kact = VE.val
        se_kact = SE.val
        for mc in range(FC):
            slot = mc % 2
            ppsum = pa if mc % 2 == 0 else pb

            for kc in range(FC):
                def mmk1(t, slot=slot, ppsum=ppsum, kc=kc):
                    r = None
                    for g in range(4):
                        r = t.matmul(ppsum[:, g * 512:(g + 1) * 512],
                                     wbufA[:, slot, kc, :],
                                     xk_t[:, kc, g * 512:(g + 1) * 512],
                                     start=(kc == 0), stop=(kc == FC - 1))
                    return r
                w = [(VE, max(mix_k_done[kc], ve_kact))]
                if kc == 0:
                    w = [wk_gd[mc], (VE, max(mix_k_done[0], ve_kact)),
                         (SE, se_kact)]
                E.op("tensor", mmk1, waits=w,
                     incs=([(TE, 1)] if kc == FC - 1 else []))
            wk_last_te[slot] = TE.val
            te_k = TE.val
            if mc + 2 < FC:
                wk_gd[mc + 2] = wload(wbufA, slot, wk0[mc + 2], wk_last_te[slot],
                                      WA)
            for hf in range(2):
                a, b = hf * 1024, (hf + 1) * 1024
                E.op("vector", lambda e, ppsum=ppsum, a=a, b=b: e.tensor_scalar(
                    rs_t[:, :], ppsum[:, a:b], 0.0, None, ALU.max),
                     waits=[(TE, te_k), (SE, se_kact)], incs=[(VE, 1)])
                v_r = VE.val
                E.op("scalar", lambda e, mc=mc, a=a, b=b: e.activation(
                    kact_t[:, mc, a:b], rs_t[:, :], AF.Square),
                     waits=[(VE, v_r)], incs=[(SE, 1)])
                se_kact = SE.val
            ve_kact = VE.val
            pp_pump(8)

        wr_gd = [None] * FC
        wv_gd = [None] * FC
        wrv_last_te = [0, 0]

        def wload_rv(slot, mc, wait_te):
            wr_gd[mc] = wload(wbufB, slot, wr0[mc], wait_te, WB)
            wv_gd[mc] = wload(wbufA, slot, wv0[mc], wait_te, WA)

        wload_rv(0, 0, wk_last_te[0])
        wload_rv(1, 1, wk_last_te[1])
        v_done = VE.val
        se_done = SE.val
        for mc in range(FC):
            slot = mc % 2
            for hf in range(2):
                a, b = hf * 1024, (hf + 1) * 1024

                def mmr(t, slot=slot, a=a):
                    r = None
                    for kc in range(FC):
                        for g in range(2):
                            r = t.matmul(pa[:, a + g * 512:a + (g + 1) * 512],
                                         wbufB[:, slot, kc, :],
                                         xr_t[:, kc, a + g * 512:a + (g + 1) * 512],
                                         start=(kc == 0), stop=(kc == FC - 1))
                    return r
                E.op("tensor", mmr,
                     waits=[wr_gd[mc], (VE, v_done), (SE, se_done)],
                     incs=[(TE, 1)])

                def mmv(t, slot=slot, a=a):
                    r = None
                    for kc in range(FC):
                        for g in range(2):
                            r = t.matmul(pb[:, a + g * 512:a + (g + 1) * 512],
                                         wbufA[:, slot, kc, :],
                                         kact_t[:, kc, a + g * 512:a + (g + 1) * 512],
                                         start=(kc == 0), stop=(kc == FC - 1))
                    return r
                E.op("tensor", mmv, waits=[wv_gd[mc]], incs=[(TE, 1)])
                te_rv = TE.val
                E.op("scalar", lambda e, a=a, b=b: e.activation(
                    reps[:, 2, a:b], pa[:, a:b], AF.Sigmoid),
                     waits=[(TE, te_rv)], incs=[(SE, 1)])
                s_sig = SE.val
                E.op("vector", lambda e, a=a, b=b: e.tensor_mul(
                    rs_t[:, :], reps[:, 2, a:b], pb[:, a:b]),
                     waits=[(SE, s_sig)], incs=[(VE, 1)])
                E.op("vector", lambda e, mc=mc, a=a, b=b: e.scalar_tensor_tensor(
                    x_t[:, mc, a:b], rs_t[:, :], 0.5, x_t[:, mc, a:b],
                    ALU.mult, ALU.add),
                     incs=[(VE, 1)])
            v_done = VE.val
            se_done = SE.val
            wrv_last_te[slot] = TE.val
            if mc + 2 < FC:
                wload_rv(slot, mc + 2, wrv_last_te[slot])
            pp_pump(8)


    pp_se_guard = [0]
    pp_ve_guard = [0]
    pp_sd_guard = [0]
    pp_spill_guard = [[]]
    pp_step_ctr = [0]
    pp_entry_ve = [0]

    pp_queue = []

    def pp_pump(n):
        for _ in range(min(n, len(pp_queue))):
            pp_queue.pop(0)()

    def pp_flush():
        while pp_queue:
            pp_queue.pop(0)()

    PPNT = 64

    def queue_prepass_gamma():
        E.op("vector", lambda e: e.memset(acc_t[:, 0:2], 0.0), incs=[(VE, 1)])
        pp_ld_guard = [0, 0]
        pp_ld_guard2 = [0, 0]
        pp_acc_ve2 = [0]

        def step(i, which):
            def go(i=i, which=which):
                srcw = (w1t if which == 0 else w2t).rearrange("a k p c -> (a k) p c")
                blk = srcw[(i * 4):(i * 4 + 4), :, :]
                b = pp_step_ctr[0] % 2
                pp_step_ctr[0] += 1
                sm = PPS[b]
                E.op("sync", lambda e, blk=blk, b=b: e.dma_start(
                    ppw_t[:, b, :].rearrange("p (k c) -> p k c", k=4),
                    blk.rearrange("b p c -> p b c")),
                     waits=[(VE, pp_entry_ve[0]), (SE, pp_ld_guard2[b])],
                     incs=[(sm, 16)])
                ld = (sm, sm.val)
                scol = 2 + which
                E.op("scalar", lambda e, b=b, scol=scol: e.activation(
                    ppw_t[:, b, :], ppw_t[:, b, :], AF.Abs,
                    accum_out=acc_t[:, scol:scol + 1]),
                     waits=[ld, (VE, pp_acc_ve2[0])], incs=[(SE, 1)])
                se_abs = SE.val
                pp_ld_guard2[b] = se_abs
                E.op("vector", lambda e, which=which, scol=scol: e.tensor_add(
                    acc_t[:, which:which + 1], acc_t[:, which:which + 1],
                    acc_t[:, scol:scol + 1]),
                     waits=[(SE, se_abs)], incs=[(VE, 1)])
                pp_acc_ve2[0] = VE.val
            return go
        for i in range(PPNT):
            pp_queue.append(step(i, 0))
        for i in range(PPNT):
            pp_queue.append(step(i, 1))

    def emit_prepass_gamma_fin():
        v_acc = VE.val
        te_entry = TE.val
        E.op("tensor", lambda e: e.matmul(pb[:, 0:2], ones_f32[:, :],
                                          acc_t[:, 0:2], start=True, stop=True),
             waits=[(VE, v_acc), (TE, te_entry)], incs=[(TE, 1)])
        te_g = TE.val
        n1 = float(DFF * C)
        E.op("vector", lambda e: e.tensor_scalar(thr_t[:, 0:1], pb[:, 0:1],
                                                 0.5 / n1, 0.5 * EPS, ALU.mult,
                                                 ALU.add),
             waits=[(TE, te_g)], incs=[(VE, 1)])
        E.op("vector", lambda e: e.tensor_scalar(thr_t[:, 2:3], pb[:, 1:2],
                                                 0.5 / n1, 0.5 * EPS, ALU.mult,
                                                 ALU.add),
             incs=[(VE, 1)])
        E.op("vector", lambda e: e.tensor_scalar(thr_t[:, 1:2], thr_t[:, 0:1],
                                                 -1.0, None, ALU.mult),
             incs=[(VE, 1)])
        E.op("vector", lambda e: e.tensor_scalar(thr_t[:, 3:4], thr_t[:, 2:3],
                                                 -1.0, None, ALU.mult),
             incs=[(VE, 1)])
        pp_ve_guard[0] = VE.val

    def queue_prepass_quant():
        pp_ld_guard = [0, 0]
        pp_sp_guard = [(PQS[0], 0), (PQS[1], 0)]

        def step(i, which):
            def go(i=i, which=which):
                srcw = (w1t if which == 0 else w2t).rearrange("a k p c -> (a k) p c")
                dst = wq1s if which == 0 else wq2s
                b = pp_step_ctr[0] % 2
                pp_step_ctr[0] += 1
                sm = PPS[b]
                E.op("sync", lambda e, i=i, srcw=srcw, b=b: e.dma_start(
                    ppw_t[:, b, :].rearrange("p (k c) -> p k c", k=4),
                    srcw[(i * 4):(i * 4 + 4), :, :].rearrange("b p c -> p b c")),
                     waits=[(VE, max(pp_ld_guard[b], pp_ve_guard[0],
                                     pp_entry_ve[0]))],
                     incs=[(sm, 16)])
                ld = (sm, sm.val)
                tc = 1 if which == 0 else 3
                E.op("vector", lambda e, b=b, tc=tc: e.tensor_single_scalar(
                    ppb_t[:, :], ppw_t[:, b, :], thr_t[:, tc:tc + 1], ALU.is_lt),
                     waits=[ld], incs=[(VE, 1)])
                tcp = 0 if which == 0 else 2
                E.op("vector", lambda e, b=b, tcp=tcp: e.scalar_tensor_tensor(
                    ppq_t[:, b, :], ppw_t[:, b, :], thr_t[:, tcp:tcp + 1],
                    ppb_t[:, :], ALU.is_gt, ALU.subtract),
                     waits=[pp_sp_guard[b]], incs=[(VE, 1)])
                pp_ld_guard[b] = VE.val
                E.op("gpsimd", lambda e, dst=dst, i=i, b=b: e.dma_start(
                    dst[(i * 4):(i * 4 + 4), :, :].rearrange("b p c -> p b c"),
                    ppq_t[:, b, :].rearrange("p (k c) -> p k c", k=4)),
                     waits=[(VE, pp_ld_guard[b])], incs=[(PQS[b], 16)])
                pp_sp_guard[b] = (PQS[b], PQS[b].val)
                pp_spill_guard[0] = [(PQS[0], PQS[0].val), (PQS[1], PQS[1].val)]
            return go
        for i in range(PPNT):
            pp_queue.append(step(i, 0))
        for i in range(PPNT):
            pp_queue.append(step(i, 1))


    def emit_attn(layer):
        is0 = (layer == 0)
        s_stats = emit_ln(lnxc_t, lnxs_t, want_g=is0)
        for c in range(FC):
            for hf in range(2):
                a, b = hf * 1024, (hf + 1) * 1024
                E.op("vector", lambda e, c=c, a=a, b=b: e.tensor_sub(
                    rs_t[:, :], x_t[:, c, a:b], reps[:, 0, a:b]),
                     waits=[(SE, s_stats)], incs=[(VE, 1)])
                sl = 2 if is0 else 1
                E.op("vector", lambda e, a=a, sl=sl: e.tensor_mul(
                    rs_t[:, :], rs_t[:, :], reps[:, sl, a:a + 1024]),
                     incs=[(VE, 1)])
                if is0:
                    E.op("vector", lambda e, c=c, a=a, b=b: e.tensor_scalar(
                        xn_t[:, c, a + 1:b + 1], rs_t[:, :], gb0_t[:, c:c + 1],
                        gb0_t[:, 8 + c:9 + c], ALU.mult, ALU.add),
                         incs=[(VE, 1)])
                else:
                    E.op("vector", lambda e, c=c, a=a, b=b: e.tensor_copy(
                        xn_t[:, c, a + 1:b + 1], rs_t[:, :]),
                         incs=[(VE, 1)])
        v_xn = VE.val
        sd_tab = 0
        if is0:
            E.op("sync", lambda e: e.dma_start(ctab_t[:, :], ctab_d),
                 waits=[(PP, PP.val), (PP2, PP2.val), (VE, VE.val)],
                 incs=[(SD, 16)])
            E.op("sync", lambda e: e.dma_start(stab_view[:, :], stab_d),
                 waits=[(VE, v_xn)], incs=[(SD, 16)])
            sd_tab = SD.val
        E.op("vector", lambda e: e.memset(vt_t[:, :, :, 64:128], 0.0),
             incs=[(VE, 1)])
        E.op("vector", lambda e: e.memset(vt_t[:, :, :, 64:65], 1.0),
             incs=[(VE, 1)])
        v_vt_init = VE.val

        wsrc = wqkv0 if is0 else wqkv1
        bsrc = bqkv0_t if is0 else bqkv1_t

        gd_pair = [None] * NPAIR
        te_entry = TE.val
        E.op("gpsimd", lambda e: e.dma_start(
            wqkv_t[:, :, :], wsrc[0].rearrange("k p c -> p k c")),
             waits=[(TE, te_entry)], incs=[(WQ, 16)])
        gd_pair[0] = (WQ, WQ.val)

        v_head_done = v_vt_init
        se_head_done = SE.val
        exp_guard = [0, 0]
        ctx_read_ve = [v_vt_init, v_vt_init]
        bc_read_ve = [0]
        hq_parity = [0]
        cstg_sd = 0
        te_qkv_prev = te_entry

        for p in range(NPAIR):
            def mk_mm(colbase):
                def f(t, colbase=colbase):
                    r = None
                    for kc in range(FC):
                        for g in range(4):
                            r = t.matmul(pa[:, g * 512:(g + 1) * 512],
                                         wqkv_t[:, kc, colbase:colbase + 128],
                                         xn_t[:, kc, 1 + g * 512:1 + (g + 1) * 512],
                                         start=(kc == 0), stop=(kc == FC - 1))
                    return r
                return f
            E.op("tensor", mk_mm(0),
                 waits=[gd_pair[p], (VE, max(v_xn, v_head_done)),
                        (SE, se_head_done)],
                 incs=[(TE, 1)])
            te_q = TE.val
            E.op("vector", lambda e, p=p: e.tensor_scalar(
                qb_t[:, :], pa[:, :], bsrc[:, 0, p:p + 1], 0.125, ALU.add,
                ALU.mult),
                 waits=[(TE, te_q)] + ([(SD, sd_tab)] if is0 else []),
                 incs=[(VE, 1)])
            v_q = VE.val
            if is0:
                def rotq(t):
                    r = None
                    for g in range(4):
                        r = t.matmul(pb[:, g * 512:(g + 1) * 512], rmat_b[:, :],
                                     qb_t[:, g * 512:(g + 1) * 512],
                                     start=True, stop=True)
                    return r
                E.op("tensor", rotq, waits=[(VE, v_q)], incs=[(TE, 1)])
                te_rq = TE.val
                E.op("vector", lambda e: e.tensor_mul(pb[:, :], pb[:, :],
                                                      stab_view[:, :]),
                     waits=[(TE, te_rq)], incs=[(VE, 1)])
                E.op("vector", lambda e: e.tensor_mul(qb_t[:, :], qb_t[:, :],
                                                      ctab_t[:, :]),
                     incs=[(VE, 1)])
                E.op("vector", lambda e: e.tensor_add(qb_t[:, :], qb_t[:, :],
                                                      pb[:, :]),
                     incs=[(VE, 1)])
            v_q_done = VE.val
            E.op("tensor", mk_mm(128), waits=[(VE, v_q)], incs=[(TE, 1)])
            te_k = TE.val
            E.op("vector", lambda e, p=p: e.tensor_scalar(
                kb_t[:, :], pa[:, :], bsrc[:, 1, p:p + 1], None, ALU.add),
                 waits=[(TE, te_k)], incs=[(VE, 1)])
            v_k = VE.val
            if is0:
                def rotk(t):
                    r = None
                    for g in range(4):
                        r = t.matmul(pb[:, g * 512:(g + 1) * 512], rmat_b[:, :],
                                     kb_t[:, g * 512:(g + 1) * 512],
                                     start=True, stop=True)
                    return r
                E.op("tensor", rotk, waits=[(VE, v_k)], incs=[(TE, 1)])
                te_rk = TE.val
                E.op("vector", lambda e: e.tensor_mul(pb[:, :], pb[:, :],
                                                      stab_view[:, :]),
                     waits=[(TE, te_rk)], incs=[(VE, 1)])
                E.op("vector", lambda e: e.tensor_mul(kb_t[:, :], kb_t[:, :],
                                                      ctab_t[:, :]),
                     incs=[(VE, 1)])
                E.op("vector", lambda e: e.tensor_add(kb_t[:, :], kb_t[:, :],
                                                      pb[:, :]),
                     incs=[(VE, 1)])
            v_k_done = VE.val
            E.op("tensor", mk_mm(256), waits=[(VE, v_k)], incs=[(TE, 1)])
            te_v = TE.val
            te_qkv_prev = TE.val
            if p + 1 < NPAIR:
                E.op("gpsimd", lambda e, p=p: e.dma_start(
                    wqkv_t[:, :, :], wsrc[p + 1].rearrange("k p c -> p k c")),
                     waits=[(TE, te_qkv_prev)], incs=[(WQ, 16)])
                gd_pair[p + 1] = (WQ, WQ.val)
            for half in range(2):
                ta2 = half * 1024
                E.op("vector", lambda e, p=p, ta2=ta2: e.tensor_scalar(
                    vstg_t[:, :], pa[:, ta2:ta2 + 1024], bsrc[:, 2, p:p + 1],
                    None, ALU.add),
                     waits=[(TE, te_v)], incs=[(VE, 1)])
                v_vs = VE.val

                def vtrans(t, half=half):
                    r = None
                    for hh in range(2):
                        b0 = hh * 64
                        for kc in range(8):
                            r = t.transpose(
                                pb[:, hh * 512 + kc * 64:hh * 512 + kc * 64 + 64],
                                vstg_t[b0:b0 + 64, kc * 128:(kc + 1) * 128],
                                ident_f[b0:b0 + 64, b0:b0 + 64],
                                tile_position=(b0, 0))
                    return r
                E.op("tensor", vtrans, waits=[(VE, v_vs)], incs=[(TE, 1)])
                te_vt = TE.val
                for hh in range(2):
                    E.op("vector", lambda e, hh=hh, half=half: e.tensor_copy(
                        vt_t[:, hh, half * 8:half * 8 + 8, 0:64],
                        pb[:, hh * 512:(hh + 1) * 512]),
                         waits=[(TE, te_vt)], incs=[(VE, 1)])
            v_vt = VE.val
            te_n_prev = 0
            for src_t, outc, vguard in ((qb_t, 0, v_q_done), (kb_t, 1, v_k_done)):
                for hf in range(2):
                    a = hf * 1024
                    E.op("scalar", lambda e, src_t=src_t, a=a: e.activation(
                        qsq_t[:, :], src_t[:, a:a + 1024], AF.Square),
                         waits=[(VE, max(vguard, v_vt)), (TE, te_n_prev)],
                         incs=[(SE, 1)])
                    s_qs = SE.val

                    def mmn(t):
                        r = None
                        for g in range(2):
                            r = t.matmul(pb[:, 1024 + g * 512:1024 + (g + 1) * 512],
                                         ones_b[:, :],
                                         qsq_t[:, g * 512:(g + 1) * 512],
                                         start=True, stop=True)
                        return r
                    E.op("tensor", mmn,
                         waits=[(SE, s_qs), (VE, max(v_vt, VE.val))],
                         incs=[(TE, 1)])
                    te_n = TE.val
                    te_n_prev = te_n
                    if hf == 0:
                        E.op("vector", lambda e, outc=outc: e.reduce_max(
                            nmx_t[:, outc:outc + 1], pb[:, 1024:2048], axis=AX),
                             waits=[(TE, te_n)], incs=[(VE, 1)])
                    else:
                        E.op("vector", lambda e: e.reduce_max(
                            nmx_t[:, 2:3], pb[:, 1024:2048], axis=AX),
                             waits=[(TE, te_n)], incs=[(VE, 1)])
                        E.op("vector", lambda e, outc=outc: e.tensor_max(
                            nmx_t[:, outc:outc + 1], nmx_t[:, outc:outc + 1],
                            nmx_t[:, 2:3]),
                             incs=[(VE, 1)])
            E.op("vector", lambda e: e.tensor_mul(nmx_t[:, 2:3], nmx_t[:, 0:1],
                                                  nmx_t[:, 1:2]),
                 incs=[(VE, 1)])
            v_nm = VE.val
            E.op("scalar", lambda e: e.activation(nmx_t[:, 2:3], nmx_t[:, 2:3],
                                                  AF.Sqrt),
                 waits=[(VE, v_nm)], incs=[(SE, 1)])
            s_nm = SE.val
            E.op("vector", lambda e: e.tensor_scalar(nmx_t[:, 3:4], nmx_t[:, 2:3],
                                                     -1.0, None, ALU.mult),
                 waits=[(SE, s_nm)], incs=[(VE, 1)])
            v_negm = VE.val

            for hh in range(2):
                b0 = hh * 64
                tp = (64, 0) if b0 else None
                for qh in range(2):
                    qa = qh * 1024
                    par = hq_parity[0]
                    hq_parity[0] ^= 1
                    cbank = par * 1024
                    te_sc = [0] * 16
                    te_cx = [0] * 16
                    se_ex = [0] * 16

                    def emit_cx(kc, hh=hh, cbank=cbank, par=par):
                        def mmcx(t, kc=kc, hh=hh, cbank=cbank):
                            r = None
                            for g in range(2):
                                r = t.matmul(
                                    pb[:, cbank + g * 512:cbank + (g + 1) * 512],
                                    vt_t[:, hh, kc, :],
                                    pt_t[:, kc % 2, g * 512:(g + 1) * 512],
                                    start=(kc == 0), stop=(kc == 15))
                            return r
                        cw = [(SE, se_ex[kc])]
                        if kc == 0:
                            cw.append((VE, ctx_read_ve[par]))
                        E.op("tensor", mmcx, waits=cw, incs=[(TE, 1)])
                        te_cx[kc] = TE.val

                    for kc in range(16):
                        sbank = (kc % 2) * 1024
                        wts = [(VE, v_negm),
                               (SE, se_ex[kc - 2] if kc >= 2 else exp_guard[kc % 2])]
                        if kc == 1:
                            wts.append((VE, bc_read_ve[0]))

                        def mmsc(t, kc=kc, sbank=sbank, b0=b0, qa=qa, tp=tp):
                            r = None
                            for g in range(2):
                                r = t.matmul(
                                    pa[:, sbank + g * 512:sbank + (g + 1) * 512],
                                    kb_t[b0:b0 + 64, kc * 128:(kc + 1) * 128],
                                    qb_t[b0:b0 + 64, qa + g * 512:qa + (g + 1) * 512],
                                    start=True, stop=True, tile_position=tp)
                            return r
                        E.op("tensor", mmsc, waits=wts, incs=[(TE, 1)])
                        te_sc[kc] = TE.val
                        wts = [(TE, te_sc[kc])]
                        if kc >= 2:
                            wts.append((TE, te_cx[kc - 2]))
                        E.op("scalar", lambda e, kc=kc, sbank=sbank: e.activation(
                            pt_t[:, kc % 2, :], pa[:, sbank:sbank + 1024], AF.Exp,
                            bias=nmx_t[:, 3:4]),
                             waits=wts, incs=[(SE, 1)])
                        se_ex[kc] = SE.val
                        if kc >= 1:
                            emit_cx(kc - 1)
                    emit_cx(15)
                    exp_guard[0] = se_ex[14]
                    exp_guard[1] = se_ex[15]
                    E.op("scalar", lambda e, cbank=cbank: e.activation(
                        rs_t[64:65, :], pb[64:65, cbank:cbank + 1024], AF.Ln),
                         waits=[(TE, te_cx[15])], incs=[(SE, 1)])
                    E.op("scalar", lambda e: e.activation(
                        rinv_t[64:65, :], rs_t[64:65, :], AF.Exp, scale=-1.0),
                         incs=[(SE, 1)])
                    s_ri = SE.val

                    def mmbc(t):
                        r = None
                        for g in range(2):
                            r = t.matmul(
                                pa[0:64, 1024 + g * 512:1024 + (g + 1) * 512],
                                ones_b[64:65, 0:64],
                                rinv_t[64:65, g * 512:(g + 1) * 512],
                                start=True, stop=True, tile_position=(64, 0))
                        return r
                    E.op("tensor", mmbc,
                         waits=[(SE, s_ri), (VE, bc_read_ve[0])], incs=[(TE, 1)])
                    te_bc = TE.val
                    E.op("vector", lambda e: e.tensor_copy(
                        rs_t[0:64, :], pa[0:64, 1024:2048]),
                         waits=[(TE, te_bc)], incs=[(VE, 1)])
                    bc_read_ve[0] = VE.val
                    h = 2 * p + hh
                    cchunk, crow = h // 2, (h % 2) * 64
                    if crow == 0:
                        E.op("vector", lambda e, cchunk=cchunk, qa=qa, cbank=cbank:
                             e.tensor_mul(
                                 ctx_all[0:64, cchunk, qa:qa + 1024],
                                 pb[0:64, cbank:cbank + 1024], rs_t[0:64, :]),
                             incs=[(VE, 1)])
                    else:
                        E.op("vector", lambda e, cbank=cbank: e.tensor_mul(
                            cstg_t[:, :], pb[0:64, cbank:cbank + 1024],
                            rs_t[0:64, :]),
                             waits=[(SD, cstg_sd)], incs=[(VE, 1)])
                        v_cs = VE.val
                        E.op("sync", lambda e, cchunk=cchunk, qa=qa: e.dma_start(
                            ctx_all[64:128, cchunk, qa:qa + 1024], cstg_t[:, :]),
                             waits=[(VE, v_cs)], incs=[(SD, 16)])
                        cstg_sd = SD.val
                    ctx_read_ve[par] = VE.val
                    v_head_done = VE.val
            se_head_done = SE.val

        sd_ctx = SD.val
        v_ctx = VE.val
        se_now = SE.val
        wdst = wout0 if is0 else wo1
        bvec = bout0_t if is0 else bo1_t
        gd_oc = [None] * FC
        for s in range(2):
            E.op("gpsimd", lambda e, s=s: e.dma_start(
                wbufA[:, s, :, :], wdst[s].rearrange("k p c -> p k c")),
                 waits=[(TE, te_qkv_prev)], incs=[(WA[s], 16)])
            gd_oc[s] = (WA[s], WA[s].val)
        wo_last_te = [0, 0]
        v_ep = VE.val
        for oc in range(FC):
            slot = oc % 2
            for hf in range(2):
                a = hf * 1024

                def mmo(t, slot=slot, a=a):
                    r = None
                    for kc in range(FC):
                        for g in range(2):
                            r = t.matmul(pa[:, a + g * 512:a + (g + 1) * 512],
                                         wbufA[:, slot, kc, :],
                                         ctx_all[:, kc, a + g * 512:a + (g + 1) * 512],
                                         start=(kc == 0), stop=(kc == FC - 1))
                    return r
                E.op("tensor", mmo,
                     waits=[gd_oc[oc], (SD, sd_ctx),
                            (VE, max(v_ctx, v_ep)), (SE, se_now)],
                     incs=[(TE, 1)])
                te_o = TE.val
                if is0:
                    E.op("vector", lambda e, oc=oc, a=a: e.tensor_sub(
                        rs_t[:, :], x_t[:, oc, a:a + 1024], reps[:, 0, a:a + 1024]),
                         waits=[(TE, te_o)], incs=[(VE, 1)])
                    E.op("vector", lambda e, a=a: e.tensor_mul(
                        rs_t[:, :], rs_t[:, :], reps[:, 1, a:a + 1024]),
                         incs=[(VE, 1)])
                    E.op("vector", lambda e, oc=oc, a=a: e.scalar_tensor_tensor(
                        rs_t[:, :], pa[:, a:a + 1024], bvec[:, oc:oc + 1],
                        rs_t[:, :], ALU.add, ALU.add),
                         incs=[(VE, 1)])
                    E.op("vector", lambda e, oc=oc, a=a: e.tensor_add(
                        x_t[:, oc, a:a + 1024], x_t[:, oc, a:a + 1024], rs_t[:, :]),
                         incs=[(VE, 1)])
                else:
                    E.op("vector", lambda e, oc=oc, a=a: e.scalar_tensor_tensor(
                        x_t[:, oc, a:a + 1024], pa[:, a:a + 1024],
                        bvec[:, oc:oc + 1], x_t[:, oc, a:a + 1024],
                        ALU.add, ALU.add),
                         waits=[(TE, te_o)], incs=[(VE, 1)])
            v_ep = VE.val
            wo_last_te[slot] = TE.val
            if oc + 2 < FC:
                E.op("gpsimd", lambda e, oc=oc, slot=slot: e.dma_start(
                    wbufA[:, slot, :, :], wdst[oc + 2].rearrange("k p c -> p k c")),
                     waits=[(TE, wo_last_te[slot])], incs=[(WA[slot], 16)])
                gd_oc[oc + 2] = (WA[slot], WA[slot].val)


    w1_read_guard = [0]
    w2_read_guard = [0]

    def emit_ffn1():
        s_stats = emit_ln(lnxc_t, lnxs_t)
        for c in range(FC):
            for hf in range(2):
                a, b = hf * 1024, (hf + 1) * 1024
                E.op("vector", lambda e, c=c, a=a, b=b: e.tensor_sub(
                    rs_t[:, :], x_t[:, c, a:b], reps[:, 0, a:b]),
                     waits=[(SE, s_stats)], incs=[(VE, 1)])
                E.op("vector", lambda e, c=c, a=a, b=b: e.tensor_mul(
                    xnf_t[:, c, a:b], rs_t[:, :], reps[:, 1, a:b]),
                     incs=[(VE, 1)])
        v_xn = VE.val

        for tt in range(2):
            ta = tt * 1024
            gd_w1 = [None] * 32
            w1_last_te = [0, 0]
            te_entry = TE.val
            for s in range(2):
                E.op("gpsimd", lambda e, s=s: e.dma_start(
                    wbufA[:, s, :, :],
                    wq1s[s * 8:(s + 1) * 8, :, :].rearrange("k p c -> p k c")),
                     waits=list(pp_spill_guard[0]) +
                           [(TE, max(w1_read_guard[0], te_entry))],
                     incs=[(WA[s], 16)])
                gd_w1[s] = (WA[s], WA[s].val)
            se_h = SE.val
            ve_h = VE.val
            for oc in range(32):
                slot = oc % 2
                ph = pa if oc % 2 == 0 else pb

                def mmh(t, slot=slot, ph=ph, ta=ta):
                    r = None
                    for kc in range(FC):
                        for g in range(2):
                            r = t.matmul(ph[:, g * 512:(g + 1) * 512],
                                         wbufA[:, slot, kc, :],
                                         xnf_t[:, kc, ta + g * 512:ta + (g + 1) * 512],
                                         start=(kc == 0), stop=(kc == FC - 1))
                    return r
                E.op("tensor", mmh,
                     waits=[gd_w1[oc], (VE, max(v_xn, ve_h)), (SE, se_h)],
                     incs=[(TE, 1)])
                w1_last_te[slot] = TE.val
                te_h = TE.val
                dst = hlo_t if oc < 16 else hhi_t
                if oc % 2 == 0:
                    E.op("vector", lambda e, dst=dst, oc=oc, ph=ph: e.tensor_scalar(
                        dst[:, oc % 16, :], ph[:, 0:1024], b1_t[:, oc:oc + 1],
                        0.0, ALU.add, ALU.max),
                         waits=[(TE, te_h)], incs=[(VE, 1)])
                    ve_h = VE.val
                else:
                    E.op("scalar", lambda e, dst=dst, oc=oc, ph=ph: e.activation(
                        dst[:, oc % 16, :], ph[:, 0:1024], AF.Relu,
                        bias=b1_t[:, oc:oc + 1]),
                         waits=[(TE, te_h)], incs=[(SE, 1)])
                    se_h = SE.val
                if oc + 2 < 32:
                    E.op("gpsimd", lambda e, oc=oc, slot=slot: e.dma_start(
                        wbufA[:, slot, :, :],
                        wq1s[(oc + 2) * 8:(oc + 3) * 8, :, :]
                        .rearrange("k p c -> p k c")),
                         waits=[(TE, w1_last_te[slot])], incs=[(WA[slot], 16)])
                    gd_w1[oc + 2] = (WA[slot], WA[slot].val)
            w1_read_guard[0] = TE.val
            v_hdone = VE.val
            s_hdone = SE.val

            gd_w2 = [None] * (FC * 2)
            w2_last_te = [TE.val]
            for s in range(2):
                E.op("gpsimd", lambda e, s=s: e.dma_start(
                    w2buf[:, s, :, :],
                    wq2s[s * 16:(s + 1) * 16, :, :].rearrange("k p c -> p k c")),
                     waits=list(pp_spill_guard[0]) +
                           [(PP, PP.val), (PP2, PP2.val),
                            (TE, max(w2_read_guard[0], w1_read_guard[0])),
                            (VE, pp_w2_ve_guard[0])],
                     incs=[(W2S[s], 16)])
                gd_w2[s] = (W2S[s], W2S[s].val)
            v_ep2 = VE.val
            for mc in range(FC):
                ph = pa if mc % 2 == 0 else pb

                def mmo2(t, mc=mc, ph=ph):
                    r = None
                    for kc in range(32):
                        half, idx = kc // 16, kc % 16
                        for g in range(2):
                            r = t.matmul(ph[:, g * 512:(g + 1) * 512],
                                         w2buf[:, half, idx, :],
                                         (hlo_t if kc < 16 else hhi_t)
                                         [:, kc % 16, g * 512:(g + 1) * 512],
                                         start=(kc == 0), stop=(kc == 31))
                    return r
                E.op("tensor", mmo2,
                     waits=[gd_w2[2 * mc], gd_w2[2 * mc + 1],
                            (VE, max(v_hdone, v_ep2)), (SE, s_hdone)],
                     incs=[(TE, 1)])
                w2_last_te[0] = TE.val
                te_o2 = TE.val
                E.op("vector", lambda e, mc=mc, ph=ph: e.tensor_scalar(
                    rs_t[:, :], ph[:, 0:1024], b2_t[:, mc:mc + 1], 0.5,
                    ALU.add, ALU.mult),
                     waits=[(TE, te_o2)], incs=[(VE, 1)])
                E.op("vector", lambda e, mc=mc, ta=ta: e.tensor_add(
                    x_t[:, mc, ta:ta + 1024], x_t[:, mc, ta:ta + 1024],
                    rs_t[:, :]),
                     incs=[(VE, 1)])
                v_ep2 = VE.val
                if mc + 1 < FC:
                    for s in range(2):
                        E.op("gpsimd", lambda e, mc=mc, s=s: e.dma_start(
                            w2buf[:, s, :, :],
                            wq2s[(mc + 1) * 32 + s * 16:
                                 (mc + 1) * 32 + (s + 1) * 16, :, :]
                            .rearrange("k p c -> p k c")),
                             waits=[(TE, w2_last_te[0])], incs=[(W2S[s], 16)])
                        gd_w2[2 * (mc + 1) + s] = (W2S[s], W2S[s].val)
            w2_read_guard[0] = TE.val
            pp_w2_ve_guard[0] = 0

    pp_w2_ve_guard = [0]


    seq = []
    if n_sub >= 1:
        seq.append("ffn0a")
    if n_sub >= 2:
        seq.append("attn0")
    if n_sub >= 3:
        seq.append("ffn0b")
    if n_sub >= 4:
        seq.append("ffn1a")
    if n_sub >= 5:
        seq.append("attn1")
    if n_sub >= 6:
        seq.append("ffn1b")

    need_pp = n_sub >= 4
    for ph in seq:
        if ph == "ffn0a":
            if need_pp:
                queue_prepass_gamma()
            emit_ffn0()
            if need_pp:
                pp_flush()
                emit_prepass_gamma_fin()
        elif ph == "attn0":
            emit_attn(0)
        elif ph == "ffn0b":
            if need_pp:
                queue_prepass_quant()
            emit_ffn0()
            if need_pp:
                pp_flush()
        elif ph == "ffn1a":
            pp_w2_ve_guard[0] = E.sems["VE"].val
            emit_ffn1()
        elif ph == "attn1":
            emit_attn(1)
        elif ph == "ffn1b":
            emit_ffn1()
        E.barrier()

    v_fin, s_fin, t_fin = VE.val, SE.val, TE.val
    E.op("sync", lambda e: e.dma_start(xout.rearrange("c p t -> p c t"),
                                       x_t[:, :, :]),
         waits=[(VE, v_fin), (SE, s_fin), (TE, t_fin)], incs=[(SD, 16)])
    sd_out = SD.val
    E.op("sync", lambda e: None, waits=[(SD, sd_out)])
    E.op("gpsimd", lambda e: None, waits=[(SD, sd_out)])

    import contextlib
    with contextlib.ExitStack() as stack:
        for n in list(E.sems):
            E.sems[n].h = stack.enter_context(nc.semaphore(n.lower()))
        block = stack.enter_context(nc.Block())

        @block.tensor
        def _(t):
            for f in E.stream["tensor"]:
                f(t)

        @block.vector
        def _(v):
            for f in E.stream["vector"]:
                f(v)

        @block.scalar
        def _(s):
            for f in E.stream["scalar"]:
                f(s)

        @block.sync
        def _(s):
            for f in E.stream["sync"]:
                f(s)

        @block.gpsimd
        def _(g):
            for f in E.stream["gpsimd"]:
                f(g)

    return nc




def kernel(**inputs):
    d, xcores = _prep(inputs)
    key = N_SUB
    if key not in _CACHE:
        _CACHE[key] = build_nc(N_SUB)
    nc = _CACHE[key]
    in_maps = []
    for i in range(8):
        m = dict(d)
        m["xin"] = xcores[i]
        in_maps.append(m)
    res = run_bass_kernel_spmd(nc, in_maps, core_ids=list(range(8)))
    return np.stack([res.results[i]["xout"].reshape(C, T).T for i in range(8)],
                    axis=0)


# version 14
# speedup vs baseline: 1.0053x; 1.0053x over previous
import numpy as np
import ml_dtypes
import concourse.bass as bass
import concourse.mybir as mybir
from concourse.bass_utils import run_bass_kernel_spmd

F32 = mybir.dt.float32
BF16 = mybir.dt.bfloat16
U8 = mybir.dt.uint8
AF = mybir.ActivationFunctionType
ALU = mybir.AluOpType
AX = mybir.AxisListType.X

T = 2048
C = 1024
FC = 8
H = 16
DH = 64
NPAIR = 8
DFF = 4096
EPS = 1e-5

N_SUB = 6



def _pair_cols(p):
    h0, h1 = 2 * p, 2 * p + 1
    q0 = np.arange(h0 * DH, (h0 + 1) * DH)
    q1 = np.arange(h1 * DH, (h1 + 1) * DH)
    return q0, q1


def _prep(inputs):
    d = {}
    f4 = np.float32

    X = np.asarray(inputs["L0_attn_X"], f4)
    xcores = [np.ascontiguousarray(X[i].T).reshape(FC, 128, T) for i in range(8)]

    Win = np.asarray(inputs["L0_attn_W_in"], f4)
    bin_ = np.asarray(inputs["L0_attn_b_in"], f4)
    wq0 = np.empty((NPAIR, FC, 128, 384), f4)
    bq0 = np.empty((128, 3, NPAIR), f4)
    for p in range(NPAIR):
        q0, q1 = _pair_cols(p)
        cols = np.concatenate([q0, q1, C + q0, C + q1, 2 * C + q0, 2 * C + q1])
        wq0[p] = Win[:, cols].reshape(FC, 128, 384)
        bq0[:, 0, p] = bin_[cols[0:128]]
        bq0[:, 1, p] = bin_[cols[128:256]]
        bq0[:, 2, p] = bin_[cols[256:384]]
    d["wqkv0"] = wq0
    d["bqkv0"] = np.ascontiguousarray(bq0)

    Wout = np.asarray(inputs["L0_attn_W_out"], f4)
    d["wout0"] = np.ascontiguousarray(
        Wout.reshape(FC, 128, FC, 128).transpose(2, 0, 1, 3))
    d["bout0"] = np.ascontiguousarray(
        np.asarray(inputs["L0_attn_b_out"], f4).reshape(FC, 128).T)
    gb = np.empty((128, 16), f4)
    gb[:, 0:8] = np.asarray(inputs["L0_attn_gamma"], f4).reshape(FC, 128).T
    gb[:, 8:16] = np.asarray(inputs["L0_attn_beta"], f4).reshape(FC, 128).T
    d["gb0"] = np.ascontiguousarray(gb)

    def t_lay(W):
        return np.ascontiguousarray(
            W.reshape(FC, 128, FC, 128).transpose(0, 2, 3, 1))
    d["wr0"] = t_lay(np.asarray(inputs["L0_ffn_W_r"], f4))
    d["wk0"] = t_lay(np.asarray(inputs["L0_ffn_W_k"], f4))
    d["wv0"] = t_lay(np.asarray(inputs["L0_ffn_W_v"], f4))
    mus = np.empty((128, 16), f4)
    mus[:, 0:8] = np.asarray(inputs["L0_ffn_mu_r"], f4).reshape(FC, 128).T
    mus[:, 8:16] = np.asarray(inputs["L0_ffn_mu_k"], f4).reshape(FC, 128).T
    d["mus0"] = np.ascontiguousarray(mus)

    WqT = np.asarray(inputs["L1_attn_Wq"], f4).T
    WkT = np.asarray(inputs["L1_attn_Wk"], f4).T
    WvT = np.asarray(inputs["L1_attn_Wv"], f4).T
    bq = np.asarray(inputs["L1_attn_bq"], f4)
    bk = np.asarray(inputs["L1_attn_bk"], f4)
    bv = np.asarray(inputs["L1_attn_bv"], f4)
    wq1 = np.empty((NPAIR, FC, 128, 384), f4)
    bq1 = np.empty((128, 3, NPAIR), f4)
    for p in range(NPAIR):
        q0, q1 = _pair_cols(p)
        cols = np.concatenate([q0, q1])
        wq1[p, :, :, 0:128] = WqT[:, cols].reshape(FC, 128, 128)
        wq1[p, :, :, 128:256] = WkT[:, cols].reshape(FC, 128, 128)
        wq1[p, :, :, 256:384] = WvT[:, cols].reshape(FC, 128, 128)
        bq1[:, 0, p] = bq[cols]
        bq1[:, 1, p] = bk[cols]
        bq1[:, 2, p] = bv[cols]
    d["wqkv1"] = wq1
    d["bqkv1"] = np.ascontiguousarray(bq1)
    WoT = np.asarray(inputs["L1_attn_Wo"], f4).T
    d["wo1"] = np.ascontiguousarray(
        WoT.reshape(FC, 128, FC, 128).transpose(2, 0, 1, 3))
    d["bo1"] = np.ascontiguousarray(
        np.asarray(inputs["L1_attn_bo"], f4).reshape(FC, 128).T)

    W1 = np.asarray(inputs["L1_ffn_W1"], f4)
    d["w1t"] = np.ascontiguousarray(
        W1.reshape(32, 128, FC, 128).transpose(0, 2, 3, 1))
    W2 = np.asarray(inputs["L1_ffn_W2"], f4)
    d["w2t"] = np.ascontiguousarray(
        W2.reshape(FC, 128, 32, 128).transpose(0, 2, 3, 1))
    d["b1"] = np.ascontiguousarray(
        np.asarray(inputs["L1_ffn_b1"], f4).reshape(32, 128).T)
    d["b2"] = np.ascontiguousarray(
        np.asarray(inputs["L1_ffn_b2"], f4).reshape(FC, 128).T)

    bf = ml_dtypes.bfloat16
    d["onesb"] = np.ones((128, 128), bf)
    d["identb"] = np.eye(128, dtype=bf)
    d["identf"] = np.eye(128, dtype=f4)
    rm = np.zeros((128, 128), np.float64)
    for base in (0, 64):
        for i in range(16):
            rm[base + 2 * i + 1, base + 2 * i] = -1.0
            rm[base + 2 * i, base + 2 * i + 1] = 1.0
    d["rmatb"] = rm.astype(bf)
    cosr, sinr = _rope_tables_f32()
    ct = np.ones((128, T), f4)
    st = np.zeros((128, T), f4)
    for base in (0, 64):
        ct[base:base + 32] = cosr.T
        st[base:base + 32] = sinr.T
    d["ctab"] = np.ascontiguousarray(ct)
    d["stab"] = np.ascontiguousarray(st)
    return d, xcores


def _rope_tables_f32():
    f4 = np.float32
    invf = (1.0 / 10000.0 ** (np.arange(0, 32, 2, dtype=f4) / f4(32))).astype(f4)
    tn = np.arange(T, dtype=f4)
    freqs = np.repeat((tn[:, None] * invf[None, :]).astype(f4), 2, axis=-1)
    try:
        import jax
        import jax.numpy as jnp
        cpu = jax.devices("cpu")[0]
        with jax.default_device(cpu):
            c = np.asarray(jnp.cos(jnp.asarray(freqs)))
            s = np.asarray(jnp.sin(jnp.asarray(freqs)))
        return c.astype(f4), s.astype(f4)
    except Exception:
        f8 = np.float64
        return (np.cos(freqs.astype(f8)).astype(f4),
                np.sin(freqs.astype(f8)).astype(f4))




class Sem:
    def __init__(self, name):
        self.name, self.val, self.h = name, 0, None


class Emit:
    ENGS = ("tensor", "vector", "scalar", "sync", "gpsimd")

    def __init__(self):
        self.stream = {e: [] for e in self.ENGS}
        self.sems = {n: Sem(n) for n in (
            "SD", "GD", "TE", "VE", "SE", "GE", "BR",
            "WQ", "WA0", "WA1", "WB0", "WB1", "W20", "W21", "PP", "PP2",
            "PQ0", "PQ1")}

    AUTO = {"vector": "VE", "scalar": "SE", "gpsimd": "GE"}

    def op(self, eng, fn, waits=(), incs=()):
        waits = list(waits)
        if eng in self.AUTO:
            s = self.sems[self.AUTO[eng]]
            waits.append((s, s.val))
        waits = [(s, v) for (s, v) in waits if v > 0]
        incv = []
        for s, k in incs:
            s.val += k
            incv.append((s, k))

        def run(e):
            for s, v in waits:
                e.wait_ge(s.h, v)
            inst = fn(e)
            if incv:
                if inst is None:
                    inst = e.nop()
                for s, k in incv:
                    inst = inst.then_inc(s.h, k)
        self.stream[eng].append(run)

    def barrier(self, engines=("tensor", "vector", "scalar", "sync")):
        BR = self.sems["BR"]
        target = BR.val + len(engines)
        for e in engines:
            self.op(e, lambda eng: eng.nop(), incs=[(BR, 1)])
        for e in engines:
            self.op(e, lambda eng: None, waits=[(BR, target)])



_CACHE = {}


def build_nc(n_sub):
    nc = bass.Bass("TRN2", target_bir_lowering=False)

    def din(name, shape, dt=F32):
        return nc.dram_tensor(name, list(shape), dt, kind="ExternalInput").ap()

    xin = din("xin", (FC, 128, T))
    wqkv0 = din("wqkv0", (NPAIR, FC, 128, 384))
    bqkv0 = din("bqkv0", (128, 3, NPAIR))
    wout0 = din("wout0", (FC, FC, 128, 128))
    bout0 = din("bout0", (128, FC))
    gb0 = din("gb0", (128, 16))
    wr0 = din("wr0", (FC, FC, 128, 128))
    wk0 = din("wk0", (FC, FC, 128, 128))
    wv0 = din("wv0", (FC, FC, 128, 128))
    mus0 = din("mus0", (128, 16))
    wqkv1 = din("wqkv1", (NPAIR, FC, 128, 384))
    bqkv1 = din("bqkv1", (128, 3, NPAIR))
    wo1 = din("wo1", (FC, FC, 128, 128))
    bo1 = din("bo1", (128, FC))
    w1t = din("w1t", (32, FC, 128, 128))
    b1v = din("b1", (128, 32))
    w2t = din("w2t", (FC, 32, 128, 128))
    b2v = din("b2", (128, FC))
    ctab_d = din("ctab", (128, T))
    stab_d = din("stab", (128, T))
    onesb_d = din("onesb", (128, 128), BF16)
    identb_d = din("identb", (128, 128), BF16)
    identf_d = din("identf", (128, 128), F32)
    rmatb_d = din("rmatb", (128, 128), BF16)

    xout = nc.dram_tensor("xout", [FC, 128, T], F32, kind="ExternalOutput").ap()
    wq1s = nc.dram_tensor("wq1s", [32 * FC, 128, 128], BF16, kind="Internal").ap()
    wq2s = nc.dram_tensor("wq2s", [FC * 32, 128, 128], BF16, kind="Internal").ap()

    TOTAL = 212832
    nc.alloc_sbuf_tensor("arena", [128, TOTAL], U8)
    arena_base = None
    for a in nc.m.functions[0].allocations:
        if not isinstance(a, mybir.MemoryLocationSet):
            continue
        for ml in (a.memorylocations or []):
            if ml.name == "arena":
                arena_base = ml.addr
    assert arena_base is not None

    off = {}
    cur = 0

    def region(name, size):
        nonlocal cur
        off[name] = cur
        cur += size
    region("X", 65536)
    region("REPS", 24576)
    region("RA", 32768)
    region("RB", 32768)
    region("RC", 33536)
    region("RD", 8448)
    region("RS", 4096)
    region("RW", 8192)
    region("CONST", 2560)
    assert cur <= TOTAL, cur

    def at(name, shape, dt, reg, extra=0):
        return nc.alloc_sbuf_tensor_at(name, list(shape), dt,
                                       offset=arena_base + off[reg] + extra)

    x_t = at("x_t", (128, FC, T), F32, "X")
    reps = at("reps", (128, 3, T), F32, "REPS")
    rs_t = at("rs_t", (128, 1024), F32, "RS")

    ones_b = at("ones_b", (128, 128), BF16, "CONST", 0)
    ident_b = at("ident_b", (128, 128), BF16, "CONST", 256)
    rmat_b = at("rmat_b", (128, 128), BF16, "CONST", 512)
    bqkv0_t = at("bqkv0_t", (128, 3, NPAIR), F32, "CONST", 768)
    bqkv1_t = at("bqkv1_t", (128, 3, NPAIR), F32, "CONST", 864)
    bout0_t = at("bout0_t", (128, FC), F32, "CONST", 960)
    bo1_t = at("bo1_t", (128, FC), F32, "CONST", 992)
    gb0_t = at("gb0_t", (128, 16), F32, "CONST", 1024)
    mus0_t = at("mus0_t", (128, 16), F32, "CONST", 1088)
    b1_t = at("b1_t", (128, 32), F32, "CONST", 1152)
    b2_t = at("b2_t", (128, FC), F32, "CONST", 1280)
    thr_t = at("thr_t", (128, 8), F32, "CONST", 1312)
    acc_t = at("acc_t", (128, 8), F32, "CONST", 1344)
    nmx_t = at("nmx_t", (128, 8), F32, "CONST", 1504)
    ones_f32 = at("ones_f32", (128, 128), F32, "CONST", 1536)
    ident_f = at("ident_f", (128, 128), F32, "CONST", 2048)

    xr_t = at("xr_t", (128, FC, T), BF16, "RA")
    xk_t = at("xk_t", (128, FC, T), BF16, "RB")
    kact_t = at("kact_t", (128, FC, T), BF16, "RC")
    xnscr = at("xnscr", (128, T + 1), F32, "RD")

    ctx_all = at("ctx_all", (128, FC, T), BF16, "RA")
    qb_t = at("qb_t", (128, T), BF16, "RB", 0)
    kb_t = at("kb_t", (128, T), BF16, "RB", 4096)
    vstg_t = at("vstg_t", (128, 1024), F32, "RB", 8192)
    vt_t = at("vt_t", (128, 2, 16, 128), BF16, "RB", 12288)
    pt_t = at("pt_t", (128, 2, 1024), BF16, "RB", 20480)
    qsq_t = at("qsq_t", (128, 1024), BF16, "RB", 24576)
    rinv_t = at("rinv_t", (128, 1024), BF16, "RB", 26624)
    cstg_t = at("cstg_t", (64, 1024), BF16, "RB", 28672)
    lnxc_t = at("lnxc_t", (128, 2, T), BF16, "RB", 0)
    lnxs_t = at("lnxs_t", (128, 2, T), BF16, "RB", 16384)
    xn_t = at("xn_t", (128, FC, T + 1), BF16, "RC")
    ctab_t = at("ctab_t", (128, T), F32, "RD", 0)

    hlo_t = at("hlo_t", (128, 16, 1024), BF16, "RA")
    hhi_t = at("hhi_t", (128, 16, 1024), BF16, "RB")
    xnf_t = at("xnf_t", (128, FC, T), BF16, "RC")
    w2buf = at("w2buf", (128, 2, 16, 128), BF16, "RD")
    ppw_t = at("ppw_t", (128, 2, 512), F32, "RD", 0)
    ppb_t = at("ppb_t", (128, 512), BF16, "RD", 4096)
    ppq_t = at("ppq_t", (128, 2, 512), BF16, "RD", 5120)

    wbufA = at("wbufA", (128, 2, FC, 128), BF16, "RW", 0)
    wbufB = at("wbufB", (128, 2, FC, 128), BF16, "RW", 4096)
    wqkv_t = at("wqkv_t", (128, FC, 384), BF16, "RW", 0)
    stab_view = at("stab_view", (128, T), F32, "REPS", 2 * T * 4)

    pa = nc.alloc_psum_tensor("pa", [128, 2048], F32)
    pb = nc.alloc_psum_tensor("pb", [128, 2048], F32)

    E = Emit()
    SD, GD, TE, VE, SE, GE = (E.sems[n] for n in ("SD", "GD", "TE", "VE", "SE", "GE"))
    WQ, WA0, WA1, WB0, WB1, W20, W21, PP, PP2 = (
        E.sems[n] for n in ("WQ", "WA0", "WA1", "WB0", "WB1", "W20", "W21",
                            "PP", "PP2"))
    PPS = (PP, PP2)
    PQS = (E.sems["PQ0"], E.sems["PQ1"])
    WA = (WA0, WA1)
    WB = (WB0, WB1)
    W2S = (W20, W21)

    def dma(eng_name, out_ap, in_ap, sem=None):
        sem = SD if sem is None else sem
        E.op(eng_name, lambda e, o=out_ap, i=in_ap: e.dma_start(o, i),
             incs=[(sem, 16)])

    dma("sync", x_t[:, :, :], xin.rearrange("c p t -> p c t"))
    dma("sync", ones_b[:, :], onesb_d)
    dma("sync", ident_b[:, :], identb_d)
    dma("sync", ident_f[:, :], identf_d)
    dma("sync", rmat_b[:, :], rmatb_d)
    dma("sync", bqkv0_t[:, :, :], bqkv0)
    dma("sync", bqkv1_t[:, :, :], bqkv1)
    dma("sync", bout0_t[:, :], bout0)
    dma("sync", bo1_t[:, :], bo1)
    dma("sync", gb0_t[:, :], gb0)
    dma("sync", mus0_t[:, :], mus0)
    dma("sync", b1_t[:, :], b1v)
    dma("sync", b2_t[:, :], b2v)
    prolog_sd = SD.val

    E.op("vector", lambda e: e.memset(xn_t[:, :, 0:1], 0.0),
         waits=[(SD, prolog_sd)], incs=[(VE, 1)])
    E.op("vector", lambda e: e.memset(ones_f32[:, :], 1.0), incs=[(VE, 1)])


    def emit_ln(scr_xc, scr_xs, want_g=False):
        te0 = TE.val
        te_read = [te0, te0]
        se_sq = [0, 0]
        for c in range(FC):
            b = c % 2
            E.op("vector",
                 lambda e, c=c, b=b: e.tensor_copy(scr_xc[:, b, :], x_t[:, c, :]),
                 waits=[(TE, te_read[b])], incs=[(VE, 1)])
            v_cast = VE.val
            E.op("scalar",
                 lambda e, c=c, b=b: e.activation(scr_xs[:, b, :], scr_xc[:, b, :],
                                                  AF.Square),
                 waits=[(VE, v_cast), (TE, te_read[b])], incs=[(SE, 1)])
            se_sq[b] = SE.val

            def mm_sum(t, c=c, b=b):
                r = None
                for g in range(4):
                    r = t.matmul(pa[:, g * 512:(g + 1) * 512], ones_b[:, :],
                                 scr_xc[:, b, g * 512:(g + 1) * 512],
                                 start=(c == 0), stop=(c == FC - 1))
                return r
            E.op("tensor", mm_sum, waits=[(VE, v_cast)], incs=[(TE, 1)])

            def mm_sq(t, c=c, b=b):
                r = None
                for g in range(4):
                    r = t.matmul(pb[:, g * 512:(g + 1) * 512], ones_b[:, :],
                                 scr_xs[:, b, g * 512:(g + 1) * 512],
                                 start=(c == 0), stop=(c == FC - 1))
                return r
            E.op("tensor", mm_sq, waits=[(SE, se_sq[b])], incs=[(TE, 1)])
            te_read[b] = TE.val
        te_stats = TE.val
        E.op("vector", lambda e: e.tensor_scalar(reps[:, 0, :], pa[:, :], 1.0 / C,
                                                 None, ALU.mult),
             waits=[(TE, te_stats)], incs=[(VE, 1)])
        E.op("vector", lambda e: e.tensor_scalar(reps[:, 2, :], pb[:, :], 1.0 / C,
                                                 EPS, ALU.mult, ALU.add),
             incs=[(VE, 1)])
        E.op("vector", lambda e: e.tensor_mul(reps[:, 1, :], reps[:, 0, :],
                                              reps[:, 0, :]),
             incs=[(VE, 1)])
        E.op("vector", lambda e: e.tensor_sub(reps[:, 2, :], reps[:, 2, :],
                                              reps[:, 1, :]),
             incs=[(VE, 1)])
        v_var = VE.val
        E.op("scalar", lambda e: e.activation(reps[:, 1, :], reps[:, 2, :], AF.Ln),
             waits=[(VE, v_var)], incs=[(SE, 1)])
        if want_g:
            for hf in range(2):
                sl = slice(hf * 1024, (hf + 1) * 1024)
                E.op("scalar", lambda e, sl=sl: e.activation(
                    rs_t[:, :], reps[:, 1, sl], AF.Exp, scale=-1.0),
                     incs=[(SE, 1)])
                s_r = SE.val
                E.op("vector", lambda e: e.tensor_scalar(
                    rs_t[:, :], rs_t[:, :], -1.0, 1.0, ALU.mult, ALU.add),
                     waits=[(SE, s_r)], incs=[(VE, 1)])
                E.op("vector", lambda e: e.tensor_scalar(
                    rs_t[:, :], rs_t[:, :], EPS, 1.0, ALU.mult, ALU.add),
                     incs=[(VE, 1)])
                E.op("vector", lambda e, sl=sl: e.tensor_mul(
                    rs_t[:, :], rs_t[:, :], reps[:, 2, sl]),
                     incs=[(VE, 1)])
                v_b = VE.val
                E.op("scalar", lambda e: e.activation(rs_t[:, :], rs_t[:, :], AF.Ln),
                     waits=[(VE, v_b)], incs=[(SE, 1)])
                E.op("scalar", lambda e, sl=sl: e.activation(
                    reps[:, 2, sl], rs_t[:, :], AF.Exp, scale=-0.5),
                     incs=[(SE, 1)])
        E.op("scalar", lambda e: e.activation(reps[:, 1, :], reps[:, 1, :], AF.Exp,
                                              scale=-0.5),
             incs=[(SE, 1)])
        return SE.val


    def emit_ffn0():
        s_stats = emit_ln(lnxc_t, lnxs_t)
        E.op("vector", lambda e: e.memset(xnscr[:, 0:1], 0.0),
             waits=[(PP, PP.val), (PP2, PP2.val)] + list(pp_spill_guard[0]),
             incs=[(VE, 1)])
        for c in range(FC):
            E.op("vector", lambda e, c=c: e.tensor_sub(
                xnscr[:, 1:T + 1], x_t[:, c, :], reps[:, 0, :]),
                 waits=[(SE, s_stats)], incs=[(VE, 1)])
            E.op("vector", lambda e: e.tensor_mul(
                xnscr[:, 1:T + 1], xnscr[:, 1:T + 1], reps[:, 1, :]),
                 incs=[(VE, 1)])
            for hf in range(2):
                a, b = hf * 1024, (hf + 1) * 1024
                E.op("vector", lambda e, a=a, b=b: e.tensor_sub(
                    rs_t[:, :], xnscr[:, a + 1:b + 1], xnscr[:, a:b]),
                     incs=[(VE, 1)])
                E.op("vector", lambda e, c=c, a=a, b=b: e.scalar_tensor_tensor(
                    xr_t[:, c, a:b], rs_t[:, :], mus0_t[:, c:c + 1], xnscr[:, a:b],
                    ALU.mult, ALU.add),
                     incs=[(VE, 1)])
                E.op("vector", lambda e, c=c, a=a, b=b: e.scalar_tensor_tensor(
                    xk_t[:, c, a:b], rs_t[:, :], mus0_t[:, 8 + c:9 + c],
                    xnscr[:, a:b], ALU.mult, ALU.add),
                     incs=[(VE, 1)])
        v_mix = VE.val
        pp_entry_ve[0] = v_mix

        def wload(buf, slot, src, wait_te, sems):
            sm = sems[slot]
            E.op("gpsimd", lambda e, buf=buf, slot=slot, src=src: e.dma_start(
                buf[:, slot, :, :], src.rearrange("k p c -> p k c")),
                 waits=[(TE, wait_te)], incs=[(sm, 16)])
            return (sm, sm.val)

        wk_gd = [None] * FC
        te_entry = TE.val
        wk_gd[0] = wload(wbufA, 0, wk0[0], te_entry, WA)
        wk_gd[1] = wload(wbufA, 1, wk0[1], te_entry, WA)
        wk_last_te = [0, 0]
        ve_kact = VE.val
        se_kact = SE.val
        for mc in range(FC):
            slot = mc % 2
            ppsum = pa if mc % 2 == 0 else pb

            def mmk(t, slot=slot, ppsum=ppsum):
                r = None
                for g in range(4):
                    for kc in range(FC):
                        r = t.matmul(ppsum[:, g * 512:(g + 1) * 512],
                                     wbufA[:, slot, kc, :],
                                     xk_t[:, kc, g * 512:(g + 1) * 512],
                                     start=(kc == 0), stop=(kc == FC - 1))
                return r
            E.op("tensor", mmk,
                 waits=[wk_gd[mc], (VE, max(v_mix, ve_kact)), (SE, se_kact)],
                 incs=[(TE, 1)])
            wk_last_te[slot] = TE.val
            te_k = TE.val
            if mc + 2 < FC:
                wk_gd[mc + 2] = wload(wbufA, slot, wk0[mc + 2], wk_last_te[slot],
                                      WA)
            for hf in range(2):
                a, b = hf * 1024, (hf + 1) * 1024
                E.op("vector", lambda e, ppsum=ppsum, a=a, b=b: e.tensor_scalar(
                    rs_t[:, :], ppsum[:, a:b], 0.0, None, ALU.max),
                     waits=[(TE, te_k), (SE, se_kact)], incs=[(VE, 1)])
                v_r = VE.val
                E.op("scalar", lambda e, mc=mc, a=a, b=b: e.activation(
                    kact_t[:, mc, a:b], rs_t[:, :], AF.Square),
                     waits=[(VE, v_r)], incs=[(SE, 1)])
                se_kact = SE.val
            ve_kact = VE.val
            pp_pump(8)

        wr_gd = [None] * FC
        wv_gd = [None] * FC
        wrv_last_te = [0, 0]

        def wload_rv(slot, mc, wait_te):
            wr_gd[mc] = wload(wbufB, slot, wr0[mc], wait_te, WB)
            wv_gd[mc] = wload(wbufA, slot, wv0[mc], wait_te, WA)

        wload_rv(0, 0, wk_last_te[0])
        wload_rv(1, 1, wk_last_te[1])
        v_done = VE.val
        se_done = SE.val
        for mc in range(FC):
            slot = mc % 2
            for hf in range(2):
                a, b = hf * 1024, (hf + 1) * 1024

                def mmr(t, slot=slot, a=a):
                    r = None
                    for g in range(2):
                        for kc in range(FC):
                            r = t.matmul(pa[:, a + g * 512:a + (g + 1) * 512],
                                         wbufB[:, slot, kc, :],
                                         xr_t[:, kc, a + g * 512:a + (g + 1) * 512],
                                         start=(kc == 0), stop=(kc == FC - 1))
                    return r
                E.op("tensor", mmr,
                     waits=[wr_gd[mc], (VE, v_done), (SE, se_done)],
                     incs=[(TE, 1)])

                def mmv(t, slot=slot, a=a):
                    r = None
                    for g in range(2):
                        for kc in range(FC):
                            r = t.matmul(pb[:, a + g * 512:a + (g + 1) * 512],
                                         wbufA[:, slot, kc, :],
                                         kact_t[:, kc, a + g * 512:a + (g + 1) * 512],
                                         start=(kc == 0), stop=(kc == FC - 1))
                    return r
                E.op("tensor", mmv, waits=[wv_gd[mc]], incs=[(TE, 1)])
                te_rv = TE.val
                E.op("scalar", lambda e, a=a, b=b: e.activation(
                    reps[:, 2, a:b], pa[:, a:b], AF.Sigmoid),
                     waits=[(TE, te_rv)], incs=[(SE, 1)])
                s_sig = SE.val
                E.op("vector", lambda e, a=a, b=b: e.tensor_mul(
                    rs_t[:, :], reps[:, 2, a:b], pb[:, a:b]),
                     waits=[(SE, s_sig)], incs=[(VE, 1)])
                E.op("vector", lambda e, mc=mc, a=a, b=b: e.scalar_tensor_tensor(
                    x_t[:, mc, a:b], rs_t[:, :], 0.5, x_t[:, mc, a:b],
                    ALU.mult, ALU.add),
                     incs=[(VE, 1)])
            v_done = VE.val
            se_done = SE.val
            wrv_last_te[slot] = TE.val
            if mc + 2 < FC:
                wload_rv(slot, mc + 2, wrv_last_te[slot])
            pp_pump(8)


    pp_se_guard = [0]
    pp_ve_guard = [0]
    pp_sd_guard = [0]
    pp_spill_guard = [[]]
    pp_step_ctr = [0]
    pp_entry_ve = [0]

    pp_queue = []

    def pp_pump(n):
        for _ in range(min(n, len(pp_queue))):
            pp_queue.pop(0)()

    def pp_flush():
        while pp_queue:
            pp_queue.pop(0)()

    PPNT = 64

    def queue_prepass_gamma():
        E.op("vector", lambda e: e.memset(acc_t[:, 0:2], 0.0), incs=[(VE, 1)])
        pp_ld_guard = [0, 0]
        pp_ld_guard2 = [0, 0]
        pp_acc_ve2 = [0]

        def step(i, which):
            def go(i=i, which=which):
                srcw = (w1t if which == 0 else w2t).rearrange("a k p c -> (a k) p c")
                blk = srcw[(i * 4):(i * 4 + 4), :, :]
                b = pp_step_ctr[0] % 2
                pp_step_ctr[0] += 1
                sm = PPS[b]
                E.op("sync", lambda e, blk=blk, b=b: e.dma_start(
                    ppw_t[:, b, :].rearrange("p (k c) -> p k c", k=4),
                    blk.rearrange("b p c -> p b c")),
                     waits=[(VE, pp_entry_ve[0]), (SE, pp_ld_guard2[b])],
                     incs=[(sm, 16)])
                ld = (sm, sm.val)
                scol = 2 + which
                E.op("scalar", lambda e, b=b, scol=scol: e.activation(
                    ppw_t[:, b, :], ppw_t[:, b, :], AF.Abs,
                    accum_out=acc_t[:, scol:scol + 1]),
                     waits=[ld, (VE, pp_acc_ve2[0])], incs=[(SE, 1)])
                se_abs = SE.val
                pp_ld_guard2[b] = se_abs
                E.op("vector", lambda e, which=which, scol=scol: e.tensor_add(
                    acc_t[:, which:which + 1], acc_t[:, which:which + 1],
                    acc_t[:, scol:scol + 1]),
                     waits=[(SE, se_abs)], incs=[(VE, 1)])
                pp_acc_ve2[0] = VE.val
            return go
        for i in range(PPNT):
            pp_queue.append(step(i, 0))
        for i in range(PPNT):
            pp_queue.append(step(i, 1))

    def emit_prepass_gamma_fin():
        v_acc = VE.val
        te_entry = TE.val
        E.op("tensor", lambda e: e.matmul(pb[:, 0:2], ones_f32[:, :],
                                          acc_t[:, 0:2], start=True, stop=True),
             waits=[(VE, v_acc), (TE, te_entry)], incs=[(TE, 1)])
        te_g = TE.val
        n1 = float(DFF * C)
        E.op("vector", lambda e: e.tensor_scalar(thr_t[:, 0:1], pb[:, 0:1],
                                                 0.5 / n1, 0.5 * EPS, ALU.mult,
                                                 ALU.add),
             waits=[(TE, te_g)], incs=[(VE, 1)])
        E.op("vector", lambda e: e.tensor_scalar(thr_t[:, 2:3], pb[:, 1:2],
                                                 0.5 / n1, 0.5 * EPS, ALU.mult,
                                                 ALU.add),
             incs=[(VE, 1)])
        E.op("vector", lambda e: e.tensor_scalar(thr_t[:, 1:2], thr_t[:, 0:1],
                                                 -1.0, None, ALU.mult),
             incs=[(VE, 1)])
        E.op("vector", lambda e: e.tensor_scalar(thr_t[:, 3:4], thr_t[:, 2:3],
                                                 -1.0, None, ALU.mult),
             incs=[(VE, 1)])
        pp_ve_guard[0] = VE.val

    def queue_prepass_quant():
        pp_ld_guard = [0, 0]
        pp_sp_guard = [(PQS[0], 0), (PQS[1], 0)]

        def step(i, which):
            def go(i=i, which=which):
                srcw = (w1t if which == 0 else w2t).rearrange("a k p c -> (a k) p c")
                dst = wq1s if which == 0 else wq2s
                b = pp_step_ctr[0] % 2
                pp_step_ctr[0] += 1
                sm = PPS[b]
                E.op("sync", lambda e, i=i, srcw=srcw, b=b: e.dma_start(
                    ppw_t[:, b, :].rearrange("p (k c) -> p k c", k=4),
                    srcw[(i * 4):(i * 4 + 4), :, :].rearrange("b p c -> p b c")),
                     waits=[(VE, max(pp_ld_guard[b], pp_ve_guard[0],
                                     pp_entry_ve[0]))],
                     incs=[(sm, 16)])
                ld = (sm, sm.val)
                tc = 1 if which == 0 else 3
                E.op("vector", lambda e, b=b, tc=tc: e.tensor_single_scalar(
                    ppb_t[:, :], ppw_t[:, b, :], thr_t[:, tc:tc + 1], ALU.is_lt),
                     waits=[ld], incs=[(VE, 1)])
                tcp = 0 if which == 0 else 2
                E.op("vector", lambda e, b=b, tcp=tcp: e.scalar_tensor_tensor(
                    ppq_t[:, b, :], ppw_t[:, b, :], thr_t[:, tcp:tcp + 1],
                    ppb_t[:, :], ALU.is_gt, ALU.subtract),
                     waits=[pp_sp_guard[b]], incs=[(VE, 1)])
                pp_ld_guard[b] = VE.val
                E.op("gpsimd", lambda e, dst=dst, i=i, b=b: e.dma_start(
                    dst[(i * 4):(i * 4 + 4), :, :].rearrange("b p c -> p b c"),
                    ppq_t[:, b, :].rearrange("p (k c) -> p k c", k=4)),
                     waits=[(VE, pp_ld_guard[b])], incs=[(PQS[b], 16)])
                pp_sp_guard[b] = (PQS[b], PQS[b].val)
                pp_spill_guard[0] = [(PQS[0], PQS[0].val), (PQS[1], PQS[1].val)]
            return go
        for i in range(PPNT):
            pp_queue.append(step(i, 0))
        for i in range(PPNT):
            pp_queue.append(step(i, 1))


    def emit_attn(layer):
        is0 = (layer == 0)
        s_stats = emit_ln(lnxc_t, lnxs_t, want_g=is0)
        for c in range(FC):
            for hf in range(2):
                a, b = hf * 1024, (hf + 1) * 1024
                E.op("vector", lambda e, c=c, a=a, b=b: e.tensor_sub(
                    rs_t[:, :], x_t[:, c, a:b], reps[:, 0, a:b]),
                     waits=[(SE, s_stats)], incs=[(VE, 1)])
                sl = 2 if is0 else 1
                E.op("vector", lambda e, a=a, sl=sl: e.tensor_mul(
                    rs_t[:, :], rs_t[:, :], reps[:, sl, a:a + 1024]),
                     incs=[(VE, 1)])
                if is0:
                    E.op("vector", lambda e, c=c, a=a, b=b: e.tensor_scalar(
                        xn_t[:, c, a + 1:b + 1], rs_t[:, :], gb0_t[:, c:c + 1],
                        gb0_t[:, 8 + c:9 + c], ALU.mult, ALU.add),
                         incs=[(VE, 1)])
                else:
                    E.op("vector", lambda e, c=c, a=a, b=b: e.tensor_copy(
                        xn_t[:, c, a + 1:b + 1], rs_t[:, :]),
                         incs=[(VE, 1)])
        v_xn = VE.val
        sd_tab = 0
        if is0:
            E.op("sync", lambda e: e.dma_start(ctab_t[:, :], ctab_d),
                 waits=[(PP, PP.val), (PP2, PP2.val), (VE, VE.val)],
                 incs=[(SD, 16)])
            E.op("sync", lambda e: e.dma_start(stab_view[:, :], stab_d),
                 waits=[(VE, v_xn)], incs=[(SD, 16)])
            sd_tab = SD.val
        E.op("vector", lambda e: e.memset(vt_t[:, :, :, 64:128], 0.0),
             incs=[(VE, 1)])
        E.op("vector", lambda e: e.memset(vt_t[:, :, :, 64:65], 1.0),
             incs=[(VE, 1)])
        v_vt_init = VE.val

        wsrc = wqkv0 if is0 else wqkv1
        bsrc = bqkv0_t if is0 else bqkv1_t

        gd_pair = [None] * NPAIR
        te_entry = TE.val
        E.op("gpsimd", lambda e: e.dma_start(
            wqkv_t[:, :, :], wsrc[0].rearrange("k p c -> p k c")),
             waits=[(TE, te_entry)], incs=[(WQ, 16)])
        gd_pair[0] = (WQ, WQ.val)

        v_head_done = v_vt_init
        se_head_done = SE.val
        exp_guard = [0, 0]
        ctx_read_ve = [v_vt_init, v_vt_init]
        bc_read_ve = [0]
        hq_parity = [0]
        cstg_sd = 0
        te_qkv_prev = te_entry

        for p in range(NPAIR):
            def mk_mm(colbase):
                def f(t, colbase=colbase):
                    r = None
                    for g in range(4):
                        for kc in range(FC):
                            r = t.matmul(pa[:, g * 512:(g + 1) * 512],
                                         wqkv_t[:, kc, colbase:colbase + 128],
                                         xn_t[:, kc, 1 + g * 512:1 + (g + 1) * 512],
                                         start=(kc == 0), stop=(kc == FC - 1))
                    return r
                return f
            E.op("tensor", mk_mm(0),
                 waits=[gd_pair[p], (VE, max(v_xn, v_head_done)),
                        (SE, se_head_done)],
                 incs=[(TE, 1)])
            te_q = TE.val
            E.op("vector", lambda e, p=p: e.tensor_scalar(
                qb_t[:, :], pa[:, :], bsrc[:, 0, p:p + 1], 0.125, ALU.add,
                ALU.mult),
                 waits=[(TE, te_q)] + ([(SD, sd_tab)] if is0 else []),
                 incs=[(VE, 1)])
            v_q = VE.val
            if is0:
                def rotq(t):
                    r = None
                    for g in range(4):
                        r = t.matmul(pb[:, g * 512:(g + 1) * 512], rmat_b[:, :],
                                     qb_t[:, g * 512:(g + 1) * 512],
                                     start=True, stop=True)
                    return r
                E.op("tensor", rotq, waits=[(VE, v_q)], incs=[(TE, 1)])
                te_rq = TE.val
                E.op("vector", lambda e: e.tensor_mul(pb[:, :], pb[:, :],
                                                      stab_view[:, :]),
                     waits=[(TE, te_rq)], incs=[(VE, 1)])
                E.op("vector", lambda e: e.tensor_mul(qb_t[:, :], qb_t[:, :],
                                                      ctab_t[:, :]),
                     incs=[(VE, 1)])
                E.op("vector", lambda e: e.tensor_add(qb_t[:, :], qb_t[:, :],
                                                      pb[:, :]),
                     incs=[(VE, 1)])
            v_q_done = VE.val
            E.op("tensor", mk_mm(128), waits=[(VE, v_q)], incs=[(TE, 1)])
            te_k = TE.val
            E.op("vector", lambda e, p=p: e.tensor_scalar(
                kb_t[:, :], pa[:, :], bsrc[:, 1, p:p + 1], None, ALU.add),
                 waits=[(TE, te_k)], incs=[(VE, 1)])
            v_k = VE.val
            if is0:
                def rotk(t):
                    r = None
                    for g in range(4):
                        r = t.matmul(pb[:, g * 512:(g + 1) * 512], rmat_b[:, :],
                                     kb_t[:, g * 512:(g + 1) * 512],
                                     start=True, stop=True)
                    return r
                E.op("tensor", rotk, waits=[(VE, v_k)], incs=[(TE, 1)])
                te_rk = TE.val
                E.op("vector", lambda e: e.tensor_mul(pb[:, :], pb[:, :],
                                                      stab_view[:, :]),
                     waits=[(TE, te_rk)], incs=[(VE, 1)])
                E.op("vector", lambda e: e.tensor_mul(kb_t[:, :], kb_t[:, :],
                                                      ctab_t[:, :]),
                     incs=[(VE, 1)])
                E.op("vector", lambda e: e.tensor_add(kb_t[:, :], kb_t[:, :],
                                                      pb[:, :]),
                     incs=[(VE, 1)])
            v_k_done = VE.val
            E.op("tensor", mk_mm(256), waits=[(VE, v_k)], incs=[(TE, 1)])
            te_v = TE.val
            te_qkv_prev = TE.val
            if p + 1 < NPAIR:
                E.op("gpsimd", lambda e, p=p: e.dma_start(
                    wqkv_t[:, :, :], wsrc[p + 1].rearrange("k p c -> p k c")),
                     waits=[(TE, te_qkv_prev)], incs=[(WQ, 16)])
                gd_pair[p + 1] = (WQ, WQ.val)
            for half in range(2):
                ta2 = half * 1024
                E.op("vector", lambda e, p=p, ta2=ta2: e.tensor_scalar(
                    vstg_t[:, :], pa[:, ta2:ta2 + 1024], bsrc[:, 2, p:p + 1],
                    None, ALU.add),
                     waits=[(TE, te_v)], incs=[(VE, 1)])
                v_vs = VE.val

                def vtrans(t, half=half):
                    r = None
                    for hh in range(2):
                        b0 = hh * 64
                        for kc in range(8):
                            r = t.transpose(
                                pb[:, hh * 512 + kc * 64:hh * 512 + kc * 64 + 64],
                                vstg_t[b0:b0 + 64, kc * 128:(kc + 1) * 128],
                                ident_f[b0:b0 + 64, b0:b0 + 64],
                                tile_position=(b0, 0))
                    return r
                E.op("tensor", vtrans, waits=[(VE, v_vs)], incs=[(TE, 1)])
                te_vt = TE.val
                for hh in range(2):
                    E.op("vector", lambda e, hh=hh, half=half: e.tensor_copy(
                        vt_t[:, hh, half * 8:half * 8 + 8, 0:64],
                        pb[:, hh * 512:(hh + 1) * 512]),
                         waits=[(TE, te_vt)], incs=[(VE, 1)])
            v_vt = VE.val
            te_n_prev = 0
            for src_t, outc, vguard in ((qb_t, 0, v_q_done), (kb_t, 1, v_k_done)):
                for hf in range(2):
                    a = hf * 1024
                    E.op("scalar", lambda e, src_t=src_t, a=a: e.activation(
                        qsq_t[:, :], src_t[:, a:a + 1024], AF.Square),
                         waits=[(VE, max(vguard, v_vt)), (TE, te_n_prev)],
                         incs=[(SE, 1)])
                    s_qs = SE.val

                    def mmn(t):
                        r = None
                        for g in range(2):
                            r = t.matmul(pb[:, 1024 + g * 512:1024 + (g + 1) * 512],
                                         ones_b[:, :],
                                         qsq_t[:, g * 512:(g + 1) * 512],
                                         start=True, stop=True)
                        return r
                    E.op("tensor", mmn,
                         waits=[(SE, s_qs), (VE, max(v_vt, VE.val))],
                         incs=[(TE, 1)])
                    te_n = TE.val
                    te_n_prev = te_n
                    if hf == 0:
                        E.op("vector", lambda e, outc=outc: e.reduce_max(
                            nmx_t[:, outc:outc + 1], pb[:, 1024:2048], axis=AX),
                             waits=[(TE, te_n)], incs=[(VE, 1)])
                    else:
                        E.op("vector", lambda e: e.reduce_max(
                            nmx_t[:, 2:3], pb[:, 1024:2048], axis=AX),
                             waits=[(TE, te_n)], incs=[(VE, 1)])
                        E.op("vector", lambda e, outc=outc: e.tensor_max(
                            nmx_t[:, outc:outc + 1], nmx_t[:, outc:outc + 1],
                            nmx_t[:, 2:3]),
                             incs=[(VE, 1)])
            E.op("vector", lambda e: e.tensor_mul(nmx_t[:, 2:3], nmx_t[:, 0:1],
                                                  nmx_t[:, 1:2]),
                 incs=[(VE, 1)])
            v_nm = VE.val
            E.op("scalar", lambda e: e.activation(nmx_t[:, 2:3], nmx_t[:, 2:3],
                                                  AF.Sqrt),
                 waits=[(VE, v_nm)], incs=[(SE, 1)])
            s_nm = SE.val
            E.op("vector", lambda e: e.tensor_scalar(nmx_t[:, 3:4], nmx_t[:, 2:3],
                                                     -1.0, None, ALU.mult),
                 waits=[(SE, s_nm)], incs=[(VE, 1)])
            v_negm = VE.val

            for hh in range(2):
                b0 = hh * 64
                tp = (64, 0) if b0 else None
                for qh in range(2):
                    qa = qh * 1024
                    par = hq_parity[0]
                    hq_parity[0] ^= 1
                    cbank = par * 1024
                    te_sc = [0] * 16
                    te_cx = [0] * 16
                    se_ex = [0] * 16

                    def emit_cx(kc, hh=hh, cbank=cbank, par=par):
                        def mmcx(t, kc=kc, hh=hh, cbank=cbank):
                            r = None
                            for g in range(2):
                                r = t.matmul(
                                    pb[:, cbank + g * 512:cbank + (g + 1) * 512],
                                    vt_t[:, hh, kc, :],
                                    pt_t[:, kc % 2, g * 512:(g + 1) * 512],
                                    start=(kc == 0), stop=(kc == 15))
                            return r
                        cw = [(SE, se_ex[kc])]
                        if kc == 0:
                            cw.append((VE, ctx_read_ve[par]))
                        E.op("tensor", mmcx, waits=cw, incs=[(TE, 1)])
                        te_cx[kc] = TE.val

                    for kc in range(16):
                        sbank = (kc % 2) * 1024
                        wts = [(VE, v_negm),
                               (SE, se_ex[kc - 2] if kc >= 2 else exp_guard[kc % 2])]
                        if kc == 1:
                            wts.append((VE, bc_read_ve[0]))

                        def mmsc(t, kc=kc, sbank=sbank, b0=b0, qa=qa, tp=tp):
                            r = None
                            for g in range(2):
                                r = t.matmul(
                                    pa[:, sbank + g * 512:sbank + (g + 1) * 512],
                                    kb_t[b0:b0 + 64, kc * 128:(kc + 1) * 128],
                                    qb_t[b0:b0 + 64, qa + g * 512:qa + (g + 1) * 512],
                                    start=True, stop=True, tile_position=tp)
                            return r
                        E.op("tensor", mmsc, waits=wts, incs=[(TE, 1)])
                        te_sc[kc] = TE.val
                        wts = [(TE, te_sc[kc])]
                        if kc >= 2:
                            wts.append((TE, te_cx[kc - 2]))
                        E.op("scalar", lambda e, kc=kc, sbank=sbank: e.activation(
                            pt_t[:, kc % 2, :], pa[:, sbank:sbank + 1024], AF.Exp,
                            bias=nmx_t[:, 3:4]),
                             waits=wts, incs=[(SE, 1)])
                        se_ex[kc] = SE.val
                        if kc >= 1:
                            emit_cx(kc - 1)
                    emit_cx(15)
                    exp_guard[0] = se_ex[14]
                    exp_guard[1] = se_ex[15]
                    E.op("scalar", lambda e, cbank=cbank: e.activation(
                        rs_t[64:65, :], pb[64:65, cbank:cbank + 1024], AF.Ln),
                         waits=[(TE, te_cx[15])], incs=[(SE, 1)])
                    E.op("scalar", lambda e: e.activation(
                        rinv_t[64:65, :], rs_t[64:65, :], AF.Exp, scale=-1.0),
                         incs=[(SE, 1)])
                    s_ri = SE.val

                    def mmbc(t):
                        r = None
                        for g in range(2):
                            r = t.matmul(
                                pa[0:64, 1024 + g * 512:1024 + (g + 1) * 512],
                                ones_b[64:65, 0:64],
                                rinv_t[64:65, g * 512:(g + 1) * 512],
                                start=True, stop=True, tile_position=(64, 0))
                        return r
                    E.op("tensor", mmbc,
                         waits=[(SE, s_ri), (VE, bc_read_ve[0])], incs=[(TE, 1)])
                    te_bc = TE.val
                    E.op("vector", lambda e: e.tensor_copy(
                        rs_t[0:64, :], pa[0:64, 1024:2048]),
                         waits=[(TE, te_bc)], incs=[(VE, 1)])
                    bc_read_ve[0] = VE.val
                    h = 2 * p + hh
                    cchunk, crow = h // 2, (h % 2) * 64
                    if crow == 0:
                        E.op("vector", lambda e, cchunk=cchunk, qa=qa, cbank=cbank:
                             e.tensor_mul(
                                 ctx_all[0:64, cchunk, qa:qa + 1024],
                                 pb[0:64, cbank:cbank + 1024], rs_t[0:64, :]),
                             incs=[(VE, 1)])
                    else:
                        E.op("vector", lambda e, cbank=cbank: e.tensor_mul(
                            cstg_t[:, :], pb[0:64, cbank:cbank + 1024],
                            rs_t[0:64, :]),
                             waits=[(SD, cstg_sd)], incs=[(VE, 1)])
                        v_cs = VE.val
                        E.op("sync", lambda e, cchunk=cchunk, qa=qa: e.dma_start(
                            ctx_all[64:128, cchunk, qa:qa + 1024], cstg_t[:, :]),
                             waits=[(VE, v_cs)], incs=[(SD, 16)])
                        cstg_sd = SD.val
                    ctx_read_ve[par] = VE.val
                    v_head_done = VE.val
            se_head_done = SE.val

        sd_ctx = SD.val
        v_ctx = VE.val
        se_now = SE.val
        wdst = wout0 if is0 else wo1
        bvec = bout0_t if is0 else bo1_t
        gd_oc = [None] * FC
        for s in range(2):
            E.op("gpsimd", lambda e, s=s: e.dma_start(
                wbufA[:, s, :, :], wdst[s].rearrange("k p c -> p k c")),
                 waits=[(TE, te_qkv_prev)], incs=[(WA[s], 16)])
            gd_oc[s] = (WA[s], WA[s].val)
        wo_last_te = [0, 0]
        v_ep = VE.val
        for oc in range(FC):
            slot = oc % 2
            for hf in range(2):
                a = hf * 1024

                def mmo(t, slot=slot, a=a):
                    r = None
                    for g in range(2):
                        for kc in range(FC):
                            r = t.matmul(pa[:, a + g * 512:a + (g + 1) * 512],
                                         wbufA[:, slot, kc, :],
                                         ctx_all[:, kc, a + g * 512:a + (g + 1) * 512],
                                         start=(kc == 0), stop=(kc == FC - 1))
                    return r
                E.op("tensor", mmo,
                     waits=[gd_oc[oc], (SD, sd_ctx),
                            (VE, max(v_ctx, v_ep)), (SE, se_now)],
                     incs=[(TE, 1)])
                te_o = TE.val
                if is0:
                    E.op("vector", lambda e, oc=oc, a=a: e.tensor_sub(
                        rs_t[:, :], x_t[:, oc, a:a + 1024], reps[:, 0, a:a + 1024]),
                         waits=[(TE, te_o)], incs=[(VE, 1)])
                    E.op("vector", lambda e, a=a: e.tensor_mul(
                        rs_t[:, :], rs_t[:, :], reps[:, 1, a:a + 1024]),
                         incs=[(VE, 1)])
                    E.op("vector", lambda e, oc=oc, a=a: e.scalar_tensor_tensor(
                        rs_t[:, :], pa[:, a:a + 1024], bvec[:, oc:oc + 1],
                        rs_t[:, :], ALU.add, ALU.add),
                         incs=[(VE, 1)])
                    E.op("vector", lambda e, oc=oc, a=a: e.tensor_add(
                        x_t[:, oc, a:a + 1024], x_t[:, oc, a:a + 1024], rs_t[:, :]),
                         incs=[(VE, 1)])
                else:
                    E.op("vector", lambda e, oc=oc, a=a: e.scalar_tensor_tensor(
                        x_t[:, oc, a:a + 1024], pa[:, a:a + 1024],
                        bvec[:, oc:oc + 1], x_t[:, oc, a:a + 1024],
                        ALU.add, ALU.add),
                         waits=[(TE, te_o)], incs=[(VE, 1)])
            v_ep = VE.val
            wo_last_te[slot] = TE.val
            if oc + 2 < FC:
                E.op("gpsimd", lambda e, oc=oc, slot=slot: e.dma_start(
                    wbufA[:, slot, :, :], wdst[oc + 2].rearrange("k p c -> p k c")),
                     waits=[(TE, wo_last_te[slot])], incs=[(WA[slot], 16)])
                gd_oc[oc + 2] = (WA[slot], WA[slot].val)


    w1_read_guard = [0]
    w2_read_guard = [0]

    def emit_ffn1(stream_out=False):
        s_stats = emit_ln(lnxc_t, lnxs_t)
        for c in range(FC):
            for hf in range(2):
                a, b = hf * 1024, (hf + 1) * 1024
                E.op("vector", lambda e, c=c, a=a, b=b: e.tensor_sub(
                    rs_t[:, :], x_t[:, c, a:b], reps[:, 0, a:b]),
                     waits=[(SE, s_stats)], incs=[(VE, 1)])
                E.op("vector", lambda e, c=c, a=a, b=b: e.tensor_mul(
                    xnf_t[:, c, a:b], rs_t[:, :], reps[:, 1, a:b]),
                     incs=[(VE, 1)])
        v_xn = VE.val

        for tt in range(2):
            ta = tt * 1024
            gd_w1 = [None] * 32
            w1_last_te = [0, 0]
            te_entry = TE.val
            for s in range(2):
                E.op("gpsimd", lambda e, s=s: e.dma_start(
                    wbufA[:, s, :, :],
                    wq1s[s * 8:(s + 1) * 8, :, :].rearrange("k p c -> p k c")),
                     waits=list(pp_spill_guard[0]) +
                           [(TE, max(w1_read_guard[0], te_entry))],
                     incs=[(WA[s], 16)])
                gd_w1[s] = (WA[s], WA[s].val)
            se_h = SE.val
            ve_h = VE.val
            for oc in range(32):
                slot = oc % 2
                ph = pa if oc % 2 == 0 else pb

                def mmh(t, slot=slot, ph=ph, ta=ta):
                    r = None
                    for g in range(2):
                        for kc in range(FC):
                            r = t.matmul(ph[:, g * 512:(g + 1) * 512],
                                         wbufA[:, slot, kc, :],
                                         xnf_t[:, kc, ta + g * 512:ta + (g + 1) * 512],
                                         start=(kc == 0), stop=(kc == FC - 1))
                    return r
                E.op("tensor", mmh,
                     waits=[gd_w1[oc], (VE, max(v_xn, ve_h)), (SE, se_h)],
                     incs=[(TE, 1)])
                w1_last_te[slot] = TE.val
                te_h = TE.val
                dst = hlo_t if oc < 16 else hhi_t
                if oc % 2 == 0:
                    E.op("vector", lambda e, dst=dst, oc=oc, ph=ph: e.tensor_scalar(
                        dst[:, oc % 16, :], ph[:, 0:1024], b1_t[:, oc:oc + 1],
                        0.0, ALU.add, ALU.max),
                         waits=[(TE, te_h)], incs=[(VE, 1)])
                    ve_h = VE.val
                else:
                    E.op("scalar", lambda e, dst=dst, oc=oc, ph=ph: e.activation(
                        dst[:, oc % 16, :], ph[:, 0:1024], AF.Relu,
                        bias=b1_t[:, oc:oc + 1]),
                         waits=[(TE, te_h)], incs=[(SE, 1)])
                    se_h = SE.val
                if oc + 2 < 32:
                    E.op("gpsimd", lambda e, oc=oc, slot=slot: e.dma_start(
                        wbufA[:, slot, :, :],
                        wq1s[(oc + 2) * 8:(oc + 3) * 8, :, :]
                        .rearrange("k p c -> p k c")),
                         waits=[(TE, w1_last_te[slot])], incs=[(WA[slot], 16)])
                    gd_w1[oc + 2] = (WA[slot], WA[slot].val)
            w1_read_guard[0] = TE.val
            v_hdone = VE.val
            s_hdone = SE.val

            gd_w2 = [None] * (FC * 2)
            w2_last_te = [TE.val]
            for s in range(2):
                E.op("gpsimd", lambda e, s=s: e.dma_start(
                    w2buf[:, s, :, :],
                    wq2s[s * 16:(s + 1) * 16, :, :].rearrange("k p c -> p k c")),
                     waits=list(pp_spill_guard[0]) +
                           [(PP, PP.val), (PP2, PP2.val),
                            (TE, max(w2_read_guard[0], w1_read_guard[0])),
                            (VE, pp_w2_ve_guard[0])],
                     incs=[(W2S[s], 16)])
                gd_w2[s] = (W2S[s], W2S[s].val)
            v_ep2 = VE.val
            for mc in range(FC):
                ph = pa if mc % 2 == 0 else pb

                def mmo2(t, mc=mc, ph=ph):
                    r = None
                    for g in range(2):
                        for kc in range(32):
                            half, idx = kc // 16, kc % 16
                            r = t.matmul(ph[:, g * 512:(g + 1) * 512],
                                         w2buf[:, half, idx, :],
                                         (hlo_t if kc < 16 else hhi_t)
                                         [:, kc % 16, g * 512:(g + 1) * 512],
                                         start=(kc == 0), stop=(kc == 31))
                    return r
                E.op("tensor", mmo2,
                     waits=[gd_w2[2 * mc], gd_w2[2 * mc + 1],
                            (VE, max(v_hdone, v_ep2)), (SE, s_hdone)],
                     incs=[(TE, 1)])
                w2_last_te[0] = TE.val
                te_o2 = TE.val
                E.op("vector", lambda e, mc=mc, ph=ph: e.tensor_scalar(
                    rs_t[:, :], ph[:, 0:1024], b2_t[:, mc:mc + 1], 0.5,
                    ALU.add, ALU.mult),
                     waits=[(TE, te_o2)], incs=[(VE, 1)])
                E.op("vector", lambda e, mc=mc, ta=ta: e.tensor_add(
                    x_t[:, mc, ta:ta + 1024], x_t[:, mc, ta:ta + 1024],
                    rs_t[:, :]),
                     incs=[(VE, 1)])
                v_ep2 = VE.val
                if stream_out:
                    E.op("sync", lambda e, mc=mc, ta=ta: e.dma_start(
                        xout.rearrange("c p t -> p c t")[:, mc, ta:ta + 1024],
                        x_t[:, mc, ta:ta + 1024]),
                         waits=[(VE, v_ep2)], incs=[(SD, 16)])
                if mc + 1 < FC:
                    for s in range(2):
                        E.op("gpsimd", lambda e, mc=mc, s=s: e.dma_start(
                            w2buf[:, s, :, :],
                            wq2s[(mc + 1) * 32 + s * 16:
                                 (mc + 1) * 32 + (s + 1) * 16, :, :]
                            .rearrange("k p c -> p k c")),
                             waits=[(TE, w2_last_te[0])], incs=[(W2S[s], 16)])
                        gd_w2[2 * (mc + 1) + s] = (W2S[s], W2S[s].val)
            w2_read_guard[0] = TE.val
            pp_w2_ve_guard[0] = 0

    pp_w2_ve_guard = [0]


    seq = []
    if n_sub >= 1:
        seq.append("ffn0a")
    if n_sub >= 2:
        seq.append("attn0")
    if n_sub >= 3:
        seq.append("ffn0b")
    if n_sub >= 4:
        seq.append("ffn1a")
    if n_sub >= 5:
        seq.append("attn1")
    if n_sub >= 6:
        seq.append("ffn1b")

    need_pp = n_sub >= 4
    for ph in seq:
        if ph == "ffn0a":
            if need_pp:
                queue_prepass_gamma()
            emit_ffn0()
            if need_pp:
                pp_flush()
                emit_prepass_gamma_fin()
        elif ph == "attn0":
            emit_attn(0)
        elif ph == "ffn0b":
            if need_pp:
                queue_prepass_quant()
            emit_ffn0()
            if need_pp:
                pp_flush()
        elif ph == "ffn1a":
            pp_w2_ve_guard[0] = E.sems["VE"].val
            emit_ffn1()
        elif ph == "attn1":
            emit_attn(1)
        elif ph == "ffn1b":
            emit_ffn1(stream_out=True)
        E.barrier()

    if n_sub < 6:
        v_fin, s_fin, t_fin = VE.val, SE.val, TE.val
        E.op("sync", lambda e: e.dma_start(xout.rearrange("c p t -> p c t"),
                                           x_t[:, :, :]),
             waits=[(VE, v_fin), (SE, s_fin), (TE, t_fin)], incs=[(SD, 16)])
    sd_out = SD.val
    E.op("sync", lambda e: None, waits=[(SD, sd_out)])
    E.op("gpsimd", lambda e: None, waits=[(SD, sd_out)])

    import contextlib
    with contextlib.ExitStack() as stack:
        for n in list(E.sems):
            E.sems[n].h = stack.enter_context(nc.semaphore(n.lower()))
        block = stack.enter_context(nc.Block())

        @block.tensor
        def _(t):
            for f in E.stream["tensor"]:
                f(t)

        @block.vector
        def _(v):
            for f in E.stream["vector"]:
                f(v)

        @block.scalar
        def _(s):
            for f in E.stream["scalar"]:
                f(s)

        @block.sync
        def _(s):
            for f in E.stream["sync"]:
                f(s)

        @block.gpsimd
        def _(g):
            for f in E.stream["gpsimd"]:
                f(g)

    return nc




def kernel(**inputs):
    d, xcores = _prep(inputs)
    key = N_SUB
    if key not in _CACHE:
        _CACHE[key] = build_nc(N_SUB)
    nc = _CACHE[key]
    in_maps = []
    for i in range(8):
        m = dict(d)
        m["xin"] = xcores[i]
        in_maps.append(m)
    res = run_bass_kernel_spmd(nc, in_maps, core_ids=list(range(8)))
    return np.stack([res.results[i]["xout"].reshape(C, T).T for i in range(8)],
                    axis=0)


# version 15
# speedup vs baseline: 1.0104x; 1.0050x over previous
import numpy as np
import ml_dtypes
import concourse.bass as bass
import concourse.mybir as mybir
from concourse.bass_utils import run_bass_kernel_spmd

F32 = mybir.dt.float32
BF16 = mybir.dt.bfloat16
U8 = mybir.dt.uint8
AF = mybir.ActivationFunctionType
ALU = mybir.AluOpType
AX = mybir.AxisListType.X

T = 2048
C = 1024
FC = 8
H = 16
DH = 64
NPAIR = 8
DFF = 4096
EPS = 1e-5

N_SUB = 6



def _pair_cols(p):
    h0, h1 = 2 * p, 2 * p + 1
    q0 = np.arange(h0 * DH, (h0 + 1) * DH)
    q1 = np.arange(h1 * DH, (h1 + 1) * DH)
    return q0, q1


def _prep(inputs):
    d = {}
    f4 = np.float32

    X = np.asarray(inputs["L0_attn_X"], f4)
    xcores = [np.ascontiguousarray(X[i].T).reshape(FC, 128, T) for i in range(8)]

    Win = np.asarray(inputs["L0_attn_W_in"], f4)
    bin_ = np.asarray(inputs["L0_attn_b_in"], f4)
    wq0 = np.empty((NPAIR, FC, 128, 384), f4)
    bq0 = np.empty((128, 3, NPAIR), f4)
    for p in range(NPAIR):
        q0, q1 = _pair_cols(p)
        cols = np.concatenate([q0, q1, C + q0, C + q1, 2 * C + q0, 2 * C + q1])
        wq0[p] = Win[:, cols].reshape(FC, 128, 384)
        bq0[:, 0, p] = bin_[cols[0:128]]
        bq0[:, 1, p] = bin_[cols[128:256]]
        bq0[:, 2, p] = bin_[cols[256:384]]
    d["wqkv0"] = wq0
    d["bqkv0"] = np.ascontiguousarray(bq0)

    Wout = np.asarray(inputs["L0_attn_W_out"], f4)
    d["wout0"] = np.ascontiguousarray(
        Wout.reshape(FC, 128, FC, 128).transpose(2, 0, 1, 3))
    d["bout0"] = np.ascontiguousarray(
        np.asarray(inputs["L0_attn_b_out"], f4).reshape(FC, 128).T)
    gb = np.empty((128, 16), f4)
    gb[:, 0:8] = np.asarray(inputs["L0_attn_gamma"], f4).reshape(FC, 128).T
    gb[:, 8:16] = np.asarray(inputs["L0_attn_beta"], f4).reshape(FC, 128).T
    d["gb0"] = np.ascontiguousarray(gb)

    def t_lay(W):
        return np.ascontiguousarray(
            W.reshape(FC, 128, FC, 128).transpose(0, 2, 3, 1))
    d["wr0"] = t_lay(np.asarray(inputs["L0_ffn_W_r"], f4))
    d["wk0"] = t_lay(np.asarray(inputs["L0_ffn_W_k"], f4))
    d["wv0"] = t_lay(np.asarray(inputs["L0_ffn_W_v"], f4))
    mus = np.empty((128, 16), f4)
    mus[:, 0:8] = np.asarray(inputs["L0_ffn_mu_r"], f4).reshape(FC, 128).T
    mus[:, 8:16] = np.asarray(inputs["L0_ffn_mu_k"], f4).reshape(FC, 128).T
    d["mus0"] = np.ascontiguousarray(mus)

    WqT = np.asarray(inputs["L1_attn_Wq"], f4).T
    WkT = np.asarray(inputs["L1_attn_Wk"], f4).T
    WvT = np.asarray(inputs["L1_attn_Wv"], f4).T
    bq = np.asarray(inputs["L1_attn_bq"], f4)
    bk = np.asarray(inputs["L1_attn_bk"], f4)
    bv = np.asarray(inputs["L1_attn_bv"], f4)
    wq1 = np.empty((NPAIR, FC, 128, 384), f4)
    bq1 = np.empty((128, 3, NPAIR), f4)
    for p in range(NPAIR):
        q0, q1 = _pair_cols(p)
        cols = np.concatenate([q0, q1])
        wq1[p, :, :, 0:128] = WqT[:, cols].reshape(FC, 128, 128)
        wq1[p, :, :, 128:256] = WkT[:, cols].reshape(FC, 128, 128)
        wq1[p, :, :, 256:384] = WvT[:, cols].reshape(FC, 128, 128)
        bq1[:, 0, p] = bq[cols]
        bq1[:, 1, p] = bk[cols]
        bq1[:, 2, p] = bv[cols]
    d["wqkv1"] = wq1
    d["bqkv1"] = np.ascontiguousarray(bq1)
    WoT = np.asarray(inputs["L1_attn_Wo"], f4).T
    d["wo1"] = np.ascontiguousarray(
        WoT.reshape(FC, 128, FC, 128).transpose(2, 0, 1, 3))
    d["bo1"] = np.ascontiguousarray(
        np.asarray(inputs["L1_attn_bo"], f4).reshape(FC, 128).T)

    W1 = np.asarray(inputs["L1_ffn_W1"], f4)
    d["w1t"] = np.ascontiguousarray(
        W1.reshape(32, 128, FC, 128).transpose(0, 2, 3, 1))
    W2 = np.asarray(inputs["L1_ffn_W2"], f4)
    d["w2t"] = np.ascontiguousarray(
        W2.reshape(FC, 128, 32, 128).transpose(0, 2, 3, 1))
    d["b1"] = np.ascontiguousarray(
        np.asarray(inputs["L1_ffn_b1"], f4).reshape(32, 128).T)
    d["b2"] = np.ascontiguousarray(
        np.asarray(inputs["L1_ffn_b2"], f4).reshape(FC, 128).T)

    bf = ml_dtypes.bfloat16
    d["onesb"] = np.ones((128, 128), bf)
    d["identb"] = np.eye(128, dtype=bf)
    d["identf"] = np.eye(128, dtype=f4)
    rm = np.zeros((128, 128), np.float64)
    for base in (0, 64):
        for i in range(16):
            rm[base + 2 * i + 1, base + 2 * i] = -1.0
            rm[base + 2 * i, base + 2 * i + 1] = 1.0
    d["rmatb"] = rm.astype(bf)
    cosr, sinr = _rope_tables_f32()
    ct = np.ones((128, T), f4)
    st = np.zeros((128, T), f4)
    for base in (0, 64):
        ct[base:base + 32] = cosr.T
        st[base:base + 32] = sinr.T
    d["ctab"] = np.ascontiguousarray(ct)
    d["stab"] = np.ascontiguousarray(st)
    return d, xcores


def _rope_tables_f32():
    f4 = np.float32
    invf = (1.0 / 10000.0 ** (np.arange(0, 32, 2, dtype=f4) / f4(32))).astype(f4)
    tn = np.arange(T, dtype=f4)
    freqs = np.repeat((tn[:, None] * invf[None, :]).astype(f4), 2, axis=-1)
    try:
        import jax
        import jax.numpy as jnp
        cpu = jax.devices("cpu")[0]
        with jax.default_device(cpu):
            c = np.asarray(jnp.cos(jnp.asarray(freqs)))
            s = np.asarray(jnp.sin(jnp.asarray(freqs)))
        return c.astype(f4), s.astype(f4)
    except Exception:
        f8 = np.float64
        return (np.cos(freqs.astype(f8)).astype(f4),
                np.sin(freqs.astype(f8)).astype(f4))




class Sem:
    def __init__(self, name):
        self.name, self.val, self.h = name, 0, None


class Emit:
    ENGS = ("tensor", "vector", "scalar", "sync", "gpsimd")

    def __init__(self):
        self.stream = {e: [] for e in self.ENGS}
        self.sems = {n: Sem(n) for n in (
            "SD", "GD", "TE", "VE", "SE", "GE", "BR",
            "WQ", "WA0", "WA1", "WB0", "WB1", "W20", "W21", "PP", "PP2",
            "PQ0", "PQ1")}

    AUTO = {"vector": "VE", "scalar": "SE", "gpsimd": "GE"}

    def op(self, eng, fn, waits=(), incs=()):
        waits = list(waits)
        if eng in self.AUTO:
            s = self.sems[self.AUTO[eng]]
            waits.append((s, s.val))
        waits = [(s, v) for (s, v) in waits if v > 0]
        incv = []
        for s, k in incs:
            s.val += k
            incv.append((s, k))

        def run(e):
            for s, v in waits:
                e.wait_ge(s.h, v)
            inst = fn(e)
            if incv:
                if inst is None:
                    inst = e.nop()
                for s, k in incv:
                    inst = inst.then_inc(s.h, k)
        self.stream[eng].append(run)

    def barrier(self, engines=("tensor", "vector", "scalar", "sync")):
        BR = self.sems["BR"]
        target = BR.val + len(engines)
        for e in engines:
            self.op(e, lambda eng: eng.nop(), incs=[(BR, 1)])
        for e in engines:
            self.op(e, lambda eng: None, waits=[(BR, target)])



_CACHE = {}


def build_nc(n_sub):
    nc = bass.Bass("TRN2", target_bir_lowering=False)

    def din(name, shape, dt=F32):
        return nc.dram_tensor(name, list(shape), dt, kind="ExternalInput").ap()

    xin = din("xin", (FC, 128, T))
    wqkv0 = din("wqkv0", (NPAIR, FC, 128, 384))
    bqkv0 = din("bqkv0", (128, 3, NPAIR))
    wout0 = din("wout0", (FC, FC, 128, 128))
    bout0 = din("bout0", (128, FC))
    gb0 = din("gb0", (128, 16))
    wr0 = din("wr0", (FC, FC, 128, 128))
    wk0 = din("wk0", (FC, FC, 128, 128))
    wv0 = din("wv0", (FC, FC, 128, 128))
    mus0 = din("mus0", (128, 16))
    wqkv1 = din("wqkv1", (NPAIR, FC, 128, 384))
    bqkv1 = din("bqkv1", (128, 3, NPAIR))
    wo1 = din("wo1", (FC, FC, 128, 128))
    bo1 = din("bo1", (128, FC))
    w1t = din("w1t", (32, FC, 128, 128))
    b1v = din("b1", (128, 32))
    w2t = din("w2t", (FC, 32, 128, 128))
    b2v = din("b2", (128, FC))
    ctab_d = din("ctab", (128, T))
    stab_d = din("stab", (128, T))
    onesb_d = din("onesb", (128, 128), BF16)
    identb_d = din("identb", (128, 128), BF16)
    identf_d = din("identf", (128, 128), F32)
    rmatb_d = din("rmatb", (128, 128), BF16)

    xout = nc.dram_tensor("xout", [FC, 128, T], F32, kind="ExternalOutput").ap()
    wq1s = nc.dram_tensor("wq1s", [32 * FC, 128, 128], BF16, kind="Internal").ap()
    wq2s = nc.dram_tensor("wq2s", [FC * 32, 128, 128], BF16, kind="Internal").ap()

    TOTAL = 212832
    nc.alloc_sbuf_tensor("arena", [128, TOTAL], U8)
    arena_base = None
    for a in nc.m.functions[0].allocations:
        if not isinstance(a, mybir.MemoryLocationSet):
            continue
        for ml in (a.memorylocations or []):
            if ml.name == "arena":
                arena_base = ml.addr
    assert arena_base is not None

    off = {}
    cur = 0

    def region(name, size):
        nonlocal cur
        off[name] = cur
        cur += size
    region("X", 65536)
    region("REPS", 24576)
    region("RA", 32768)
    region("RB", 32768)
    region("RC", 33536)
    region("RD", 8448)
    region("RS", 4096)
    region("RW", 8192)
    region("CONST", 2560)
    assert cur <= TOTAL, cur

    def at(name, shape, dt, reg, extra=0):
        return nc.alloc_sbuf_tensor_at(name, list(shape), dt,
                                       offset=arena_base + off[reg] + extra)

    x_t = at("x_t", (128, FC, T), F32, "X")
    reps = at("reps", (128, 3, T), F32, "REPS")
    rs_t = at("rs_t", (128, 1024), F32, "RS")

    ones_b = at("ones_b", (128, 128), BF16, "CONST", 0)
    ident_b = at("ident_b", (128, 128), BF16, "CONST", 256)
    rmat_b = at("rmat_b", (128, 128), BF16, "CONST", 512)
    bqkv0_t = at("bqkv0_t", (128, 3, NPAIR), F32, "CONST", 768)
    bqkv1_t = at("bqkv1_t", (128, 3, NPAIR), F32, "CONST", 864)
    bout0_t = at("bout0_t", (128, FC), F32, "CONST", 960)
    bo1_t = at("bo1_t", (128, FC), F32, "CONST", 992)
    gb0_t = at("gb0_t", (128, 16), F32, "CONST", 1024)
    mus0_t = at("mus0_t", (128, 16), F32, "CONST", 1088)
    b1_t = at("b1_t", (128, 32), F32, "CONST", 1152)
    b2_t = at("b2_t", (128, FC), F32, "CONST", 1280)
    thr_t = at("thr_t", (128, 8), F32, "CONST", 1312)
    acc_t = at("acc_t", (128, 8), F32, "CONST", 1344)
    nmx_t = at("nmx_t", (128, 8), F32, "CONST", 1504)
    ones_f32 = at("ones_f32", (128, 128), F32, "CONST", 1536)
    ident_f = at("ident_f", (128, 128), F32, "CONST", 2048)

    xr_t = at("xr_t", (128, FC, T), BF16, "RA")
    xk_t = at("xk_t", (128, FC, T), BF16, "RB")
    kact_t = at("kact_t", (128, FC, T), BF16, "RC")
    xnscr = at("xnscr", (128, T + 1), F32, "RD")

    ctx_all = at("ctx_all", (128, FC, T), BF16, "RA")
    qb_t = at("qb_t", (128, T), BF16, "RB", 0)
    kb_t = at("kb_t", (128, T), BF16, "RB", 4096)
    vstg_t = at("vstg_t", (128, 1024), F32, "RB", 8192)
    vt_t = at("vt_t", (128, 2, 16, 128), BF16, "RB", 12288)
    pt_t = at("pt_t", (128, 2, 1024), BF16, "RB", 20480)
    qsq_t = at("qsq_t", (128, 1024), BF16, "RB", 24576)
    rinv_t = at("rinv_t", (128, 1024), BF16, "RB", 26624)
    cstg_t = at("cstg_t", (64, 1024), BF16, "RB", 28672)
    lnxc_t = at("lnxc_t", (128, 2, T), BF16, "RB", 0)
    lnxs_t = at("lnxs_t", (128, 2, T), BF16, "RB", 16384)
    xn_t = at("xn_t", (128, FC, T + 1), BF16, "RC")
    ctab_t = at("ctab_t", (128, T), F32, "RD", 0)

    hlo_t = at("hlo_t", (128, 16, 1024), BF16, "RA")
    hhi_t = at("hhi_t", (128, 16, 1024), BF16, "RB")
    xnf_t = at("xnf_t", (128, FC, T), BF16, "RC")
    w2buf = at("w2buf", (128, 2, 16, 128), BF16, "RD")
    ppw_t = at("ppw_t", (128, 2, 512), F32, "RD", 0)
    ppb_t = at("ppb_t", (128, 512), BF16, "RD", 4096)
    ppq_t = at("ppq_t", (128, 2, 512), BF16, "RD", 5120)

    wbufA = at("wbufA", (128, 2, FC, 128), BF16, "RW", 0)
    wbufB = at("wbufB", (128, 2, FC, 128), BF16, "RW", 4096)
    wqkv_t = at("wqkv_t", (128, FC, 384), BF16, "RW", 0)
    stab_view = at("stab_view", (128, T), F32, "REPS", 2 * T * 4)

    pa = nc.alloc_psum_tensor("pa", [128, 2048], F32)
    pb = nc.alloc_psum_tensor("pb", [128, 2048], F32)

    E = Emit()
    SD, GD, TE, VE, SE, GE = (E.sems[n] for n in ("SD", "GD", "TE", "VE", "SE", "GE"))
    WQ, WA0, WA1, WB0, WB1, W20, W21, PP, PP2 = (
        E.sems[n] for n in ("WQ", "WA0", "WA1", "WB0", "WB1", "W20", "W21",
                            "PP", "PP2"))
    PPS = (PP, PP2)
    PQS = (E.sems["PQ0"], E.sems["PQ1"])
    WA = (WA0, WA1)
    WB = (WB0, WB1)
    W2S = (W20, W21)

    def dma(eng_name, out_ap, in_ap, sem=None):
        sem = SD if sem is None else sem
        E.op(eng_name, lambda e, o=out_ap, i=in_ap: e.dma_start(o, i),
             incs=[(sem, 16)])

    dma("sync", x_t[:, :, :], xin.rearrange("c p t -> p c t"))
    dma("sync", ones_b[:, :], onesb_d)
    dma("sync", ident_b[:, :], identb_d)
    dma("sync", ident_f[:, :], identf_d)
    dma("sync", rmat_b[:, :], rmatb_d)
    dma("sync", bqkv0_t[:, :, :], bqkv0)
    dma("sync", bqkv1_t[:, :, :], bqkv1)
    dma("sync", bout0_t[:, :], bout0)
    dma("sync", bo1_t[:, :], bo1)
    dma("sync", gb0_t[:, :], gb0)
    dma("sync", mus0_t[:, :], mus0)
    dma("sync", b1_t[:, :], b1v)
    dma("sync", b2_t[:, :], b2v)
    prolog_sd = SD.val

    E.op("vector", lambda e: e.memset(xn_t[:, :, 0:1], 0.0),
         waits=[(SD, prolog_sd)], incs=[(VE, 1)])
    E.op("vector", lambda e: e.memset(ones_f32[:, :], 1.0), incs=[(VE, 1)])


    def emit_ln(scr_xc, scr_xs, want_g=False):
        te0 = TE.val
        te_read = [te0, te0]
        se_sq = [0, 0]
        for c in range(FC):
            b = c % 2
            E.op("vector",
                 lambda e, c=c, b=b: e.tensor_copy(scr_xc[:, b, :], x_t[:, c, :]),
                 waits=[(TE, te_read[b])], incs=[(VE, 1)])
            v_cast = VE.val
            E.op("scalar",
                 lambda e, c=c, b=b: e.activation(scr_xs[:, b, :], scr_xc[:, b, :],
                                                  AF.Square),
                 waits=[(VE, v_cast), (TE, te_read[b])], incs=[(SE, 1)])
            se_sq[b] = SE.val

            def mm_sum(t, c=c, b=b):
                r = None
                for g in range(4):
                    r = t.matmul(pa[:, g * 512:(g + 1) * 512], ones_b[:, :],
                                 scr_xc[:, b, g * 512:(g + 1) * 512],
                                 start=(c == 0), stop=(c == FC - 1))
                return r
            E.op("tensor", mm_sum, waits=[(VE, v_cast)], incs=[(TE, 1)])

            def mm_sq(t, c=c, b=b):
                r = None
                for g in range(4):
                    r = t.matmul(pb[:, g * 512:(g + 1) * 512], ones_b[:, :],
                                 scr_xs[:, b, g * 512:(g + 1) * 512],
                                 start=(c == 0), stop=(c == FC - 1))
                return r
            E.op("tensor", mm_sq, waits=[(SE, se_sq[b])], incs=[(TE, 1)])
            te_read[b] = TE.val
        te_stats = TE.val
        E.op("vector", lambda e: e.tensor_scalar(reps[:, 0, :], pa[:, :], 1.0 / C,
                                                 None, ALU.mult),
             waits=[(TE, te_stats)], incs=[(VE, 1)])
        E.op("vector", lambda e: e.tensor_scalar(reps[:, 2, :], pb[:, :], 1.0 / C,
                                                 EPS, ALU.mult, ALU.add),
             incs=[(VE, 1)])
        E.op("vector", lambda e: e.tensor_mul(reps[:, 1, :], reps[:, 0, :],
                                              reps[:, 0, :]),
             incs=[(VE, 1)])
        E.op("vector", lambda e: e.tensor_sub(reps[:, 2, :], reps[:, 2, :],
                                              reps[:, 1, :]),
             incs=[(VE, 1)])
        v_var = VE.val
        E.op("scalar", lambda e: e.activation(reps[:, 1, :], reps[:, 2, :], AF.Ln),
             waits=[(VE, v_var)], incs=[(SE, 1)])
        if want_g:
            for hf in range(2):
                sl = slice(hf * 1024, (hf + 1) * 1024)
                E.op("scalar", lambda e, sl=sl: e.activation(
                    rs_t[:, :], reps[:, 1, sl], AF.Exp, scale=-1.0),
                     incs=[(SE, 1)])
                s_r = SE.val
                E.op("vector", lambda e: e.tensor_scalar(
                    rs_t[:, :], rs_t[:, :], -1.0, 1.0, ALU.mult, ALU.add),
                     waits=[(SE, s_r)], incs=[(VE, 1)])
                E.op("vector", lambda e: e.tensor_scalar(
                    rs_t[:, :], rs_t[:, :], EPS, 1.0, ALU.mult, ALU.add),
                     incs=[(VE, 1)])
                E.op("vector", lambda e, sl=sl: e.tensor_mul(
                    rs_t[:, :], rs_t[:, :], reps[:, 2, sl]),
                     incs=[(VE, 1)])
                v_b = VE.val
                E.op("scalar", lambda e: e.activation(rs_t[:, :], rs_t[:, :], AF.Ln),
                     waits=[(VE, v_b)], incs=[(SE, 1)])
                E.op("scalar", lambda e, sl=sl: e.activation(
                    reps[:, 2, sl], rs_t[:, :], AF.Exp, scale=-0.5),
                     incs=[(SE, 1)])
        E.op("scalar", lambda e: e.activation(reps[:, 1, :], reps[:, 1, :], AF.Exp,
                                              scale=-0.5),
             incs=[(SE, 1)])
        return SE.val


    def emit_ffn0():
        s_stats = emit_ln(lnxc_t, lnxs_t)
        E.op("vector", lambda e: e.memset(xnscr[:, 0:1], 0.0),
             waits=[(PP, PP.val), (PP2, PP2.val)] + list(pp_spill_guard[0]),
             incs=[(VE, 1)])
        for c in range(FC):
            E.op("vector", lambda e, c=c: e.tensor_sub(
                xnscr[:, 1:T + 1], x_t[:, c, :], reps[:, 0, :]),
                 waits=[(SE, s_stats)], incs=[(VE, 1)])
            E.op("vector", lambda e: e.tensor_mul(
                xnscr[:, 1:T + 1], xnscr[:, 1:T + 1], reps[:, 1, :]),
                 incs=[(VE, 1)])
            for hf in range(2):
                a, b = hf * 1024, (hf + 1) * 1024
                E.op("vector", lambda e, a=a, b=b: e.tensor_sub(
                    rs_t[:, :], xnscr[:, a + 1:b + 1], xnscr[:, a:b]),
                     incs=[(VE, 1)])
                E.op("vector", lambda e, c=c, a=a, b=b: e.scalar_tensor_tensor(
                    xr_t[:, c, a:b], rs_t[:, :], mus0_t[:, c:c + 1], xnscr[:, a:b],
                    ALU.mult, ALU.add),
                     incs=[(VE, 1)])
                E.op("vector", lambda e, c=c, a=a, b=b: e.scalar_tensor_tensor(
                    xk_t[:, c, a:b], rs_t[:, :], mus0_t[:, 8 + c:9 + c],
                    xnscr[:, a:b], ALU.mult, ALU.add),
                     incs=[(VE, 1)])
        v_mix = VE.val
        pp_entry_ve[0] = v_mix

        def wload(buf, slot, src, wait_te, sems):
            sm = sems[slot]
            E.op("gpsimd", lambda e, buf=buf, slot=slot, src=src: e.dma_start(
                buf[:, slot, :, :], src.rearrange("k p c -> p k c")),
                 waits=[(TE, wait_te)], incs=[(sm, 16)])
            return (sm, sm.val)

        wk_gd = [None] * FC
        te_entry = TE.val
        wk_gd[0] = wload(wbufA, 0, wk0[0], te_entry, WA)
        wk_gd[1] = wload(wbufA, 1, wk0[1], te_entry, WA)
        wk_last_te = [0, 0]
        ve_kact = VE.val
        se_kact = SE.val
        for mc in range(FC):
            slot = mc % 2
            ppsum = pa if mc % 2 == 0 else pb

            def mmk(t, slot=slot, ppsum=ppsum):
                r = None
                for g in range(4):
                    for kc in range(FC):
                        r = t.matmul(ppsum[:, g * 512:(g + 1) * 512],
                                     wbufA[:, slot, kc, :],
                                     xk_t[:, kc, g * 512:(g + 1) * 512],
                                     start=(kc == 0), stop=(kc == FC - 1))
                return r
            E.op("tensor", mmk,
                 waits=[wk_gd[mc], (VE, max(v_mix, ve_kact)), (SE, se_kact)],
                 incs=[(TE, 1)])
            wk_last_te[slot] = TE.val
            te_k = TE.val
            if mc + 2 < FC:
                wk_gd[mc + 2] = wload(wbufA, slot, wk0[mc + 2], wk_last_te[slot],
                                      WA)
            for hf in range(2):
                a, b = hf * 1024, (hf + 1) * 1024
                E.op("scalar", lambda e, ppsum=ppsum, a=a, b=b: e.activation(
                    rs_t[:, :], ppsum[:, a:b], AF.Relu),
                     waits=[(TE, te_k)], incs=[(SE, 1)])
                E.op("scalar", lambda e, mc=mc, a=a, b=b: e.activation(
                    kact_t[:, mc, a:b], rs_t[:, :], AF.Square),
                     incs=[(SE, 1)])
                se_kact = SE.val
            ve_kact = VE.val
            pp_pump(8)

        wr_gd = [None] * FC
        wv_gd = [None] * FC
        wrv_last_te = [0, 0]

        def wload_rv(slot, mc, wait_te):
            wr_gd[mc] = wload(wbufB, slot, wr0[mc], wait_te, WB)
            wv_gd[mc] = wload(wbufA, slot, wv0[mc], wait_te, WA)

        wload_rv(0, 0, wk_last_te[0])
        wload_rv(1, 1, wk_last_te[1])
        v_done = VE.val
        se_done = SE.val
        for mc in range(FC):
            slot = mc % 2
            for hf in range(2):
                a, b = hf * 1024, (hf + 1) * 1024

                def mmr(t, slot=slot, a=a):
                    r = None
                    for g in range(2):
                        for kc in range(FC):
                            r = t.matmul(pa[:, a + g * 512:a + (g + 1) * 512],
                                         wbufB[:, slot, kc, :],
                                         xr_t[:, kc, a + g * 512:a + (g + 1) * 512],
                                         start=(kc == 0), stop=(kc == FC - 1))
                    return r
                E.op("tensor", mmr,
                     waits=[wr_gd[mc], (VE, v_done), (SE, se_done)],
                     incs=[(TE, 1)])

                def mmv(t, slot=slot, a=a):
                    r = None
                    for g in range(2):
                        for kc in range(FC):
                            r = t.matmul(pb[:, a + g * 512:a + (g + 1) * 512],
                                         wbufA[:, slot, kc, :],
                                         kact_t[:, kc, a + g * 512:a + (g + 1) * 512],
                                         start=(kc == 0), stop=(kc == FC - 1))
                    return r
                E.op("tensor", mmv, waits=[wv_gd[mc]], incs=[(TE, 1)])
                te_rv = TE.val
                E.op("scalar", lambda e, a=a, b=b: e.activation(
                    reps[:, 2, a:b], pa[:, a:b], AF.Sigmoid),
                     waits=[(TE, te_rv)], incs=[(SE, 1)])
                s_sig = SE.val
                E.op("vector", lambda e, a=a, b=b: e.tensor_mul(
                    rs_t[:, :], reps[:, 2, a:b], pb[:, a:b]),
                     waits=[(SE, s_sig)], incs=[(VE, 1)])
                E.op("vector", lambda e, mc=mc, a=a, b=b: e.scalar_tensor_tensor(
                    x_t[:, mc, a:b], rs_t[:, :], 0.5, x_t[:, mc, a:b],
                    ALU.mult, ALU.add),
                     incs=[(VE, 1)])
            v_done = VE.val
            se_done = SE.val
            wrv_last_te[slot] = TE.val
            if mc + 2 < FC:
                wload_rv(slot, mc + 2, wrv_last_te[slot])
            pp_pump(8)


    pp_se_guard = [0]
    pp_ve_guard = [0]
    pp_sd_guard = [0]
    pp_spill_guard = [[]]
    pp_step_ctr = [0]
    pp_entry_ve = [0]

    pp_queue = []

    def pp_pump(n):
        for _ in range(min(n, len(pp_queue))):
            pp_queue.pop(0)()

    def pp_flush():
        while pp_queue:
            pp_queue.pop(0)()

    PPNT = 64

    def queue_prepass_gamma():
        E.op("vector", lambda e: e.memset(acc_t[:, 0:2], 0.0), incs=[(VE, 1)])
        pp_ld_guard = [0, 0]
        pp_ld_guard2 = [0, 0]
        pp_acc_ve2 = [0]

        def step(i, which):
            def go(i=i, which=which):
                srcw = (w1t if which == 0 else w2t).rearrange("a k p c -> (a k) p c")
                blk = srcw[(i * 4):(i * 4 + 4), :, :]
                b = pp_step_ctr[0] % 2
                pp_step_ctr[0] += 1
                sm = PPS[b]
                E.op("sync", lambda e, blk=blk, b=b: e.dma_start(
                    ppw_t[:, b, :].rearrange("p (k c) -> p k c", k=4),
                    blk.rearrange("b p c -> p b c")),
                     waits=[(VE, pp_entry_ve[0]), (SE, pp_ld_guard2[b])],
                     incs=[(sm, 16)])
                ld = (sm, sm.val)
                scol = 2 + which
                E.op("scalar", lambda e, b=b, scol=scol: e.activation(
                    ppw_t[:, b, :], ppw_t[:, b, :], AF.Abs,
                    accum_out=acc_t[:, scol:scol + 1]),
                     waits=[ld, (VE, pp_acc_ve2[0])], incs=[(SE, 1)])
                se_abs = SE.val
                pp_ld_guard2[b] = se_abs
                E.op("vector", lambda e, which=which, scol=scol: e.tensor_add(
                    acc_t[:, which:which + 1], acc_t[:, which:which + 1],
                    acc_t[:, scol:scol + 1]),
                     waits=[(SE, se_abs)], incs=[(VE, 1)])
                pp_acc_ve2[0] = VE.val
            return go
        for i in range(PPNT):
            pp_queue.append(step(i, 0))
        for i in range(PPNT):
            pp_queue.append(step(i, 1))

    def emit_prepass_gamma_fin():
        v_acc = VE.val
        te_entry = TE.val
        E.op("tensor", lambda e: e.matmul(pb[:, 0:2], ones_f32[:, :],
                                          acc_t[:, 0:2], start=True, stop=True),
             waits=[(VE, v_acc), (TE, te_entry)], incs=[(TE, 1)])
        te_g = TE.val
        n1 = float(DFF * C)
        E.op("vector", lambda e: e.tensor_scalar(thr_t[:, 0:1], pb[:, 0:1],
                                                 0.5 / n1, 0.5 * EPS, ALU.mult,
                                                 ALU.add),
             waits=[(TE, te_g)], incs=[(VE, 1)])
        E.op("vector", lambda e: e.tensor_scalar(thr_t[:, 2:3], pb[:, 1:2],
                                                 0.5 / n1, 0.5 * EPS, ALU.mult,
                                                 ALU.add),
             incs=[(VE, 1)])
        E.op("vector", lambda e: e.tensor_scalar(thr_t[:, 1:2], thr_t[:, 0:1],
                                                 -1.0, None, ALU.mult),
             incs=[(VE, 1)])
        E.op("vector", lambda e: e.tensor_scalar(thr_t[:, 3:4], thr_t[:, 2:3],
                                                 -1.0, None, ALU.mult),
             incs=[(VE, 1)])
        pp_ve_guard[0] = VE.val

    def queue_prepass_quant():
        pp_ld_guard = [0, 0]
        pp_sp_guard = [(PQS[0], 0), (PQS[1], 0)]

        def step(i, which):
            def go(i=i, which=which):
                srcw = (w1t if which == 0 else w2t).rearrange("a k p c -> (a k) p c")
                dst = wq1s if which == 0 else wq2s
                b = pp_step_ctr[0] % 2
                pp_step_ctr[0] += 1
                sm = PPS[b]
                E.op("sync", lambda e, i=i, srcw=srcw, b=b: e.dma_start(
                    ppw_t[:, b, :].rearrange("p (k c) -> p k c", k=4),
                    srcw[(i * 4):(i * 4 + 4), :, :].rearrange("b p c -> p b c")),
                     waits=[(VE, max(pp_ld_guard[b], pp_ve_guard[0],
                                     pp_entry_ve[0]))],
                     incs=[(sm, 16)])
                ld = (sm, sm.val)
                tc = 1 if which == 0 else 3
                E.op("vector", lambda e, b=b, tc=tc: e.tensor_single_scalar(
                    ppb_t[:, :], ppw_t[:, b, :], thr_t[:, tc:tc + 1], ALU.is_lt),
                     waits=[ld], incs=[(VE, 1)])
                tcp = 0 if which == 0 else 2
                E.op("vector", lambda e, b=b, tcp=tcp: e.scalar_tensor_tensor(
                    ppq_t[:, b, :], ppw_t[:, b, :], thr_t[:, tcp:tcp + 1],
                    ppb_t[:, :], ALU.is_gt, ALU.subtract),
                     waits=[pp_sp_guard[b]], incs=[(VE, 1)])
                pp_ld_guard[b] = VE.val
                E.op("gpsimd", lambda e, dst=dst, i=i, b=b: e.dma_start(
                    dst[(i * 4):(i * 4 + 4), :, :].rearrange("b p c -> p b c"),
                    ppq_t[:, b, :].rearrange("p (k c) -> p k c", k=4)),
                     waits=[(VE, pp_ld_guard[b])], incs=[(PQS[b], 16)])
                pp_sp_guard[b] = (PQS[b], PQS[b].val)
                pp_spill_guard[0] = [(PQS[0], PQS[0].val), (PQS[1], PQS[1].val)]
            return go
        for i in range(PPNT):
            pp_queue.append(step(i, 0))
        for i in range(PPNT):
            pp_queue.append(step(i, 1))


    def emit_attn(layer):
        is0 = (layer == 0)
        s_stats = emit_ln(lnxc_t, lnxs_t, want_g=is0)
        for c in range(FC):
            for hf in range(2):
                a, b = hf * 1024, (hf + 1) * 1024
                E.op("vector", lambda e, c=c, a=a, b=b: e.tensor_sub(
                    rs_t[:, :], x_t[:, c, a:b], reps[:, 0, a:b]),
                     waits=[(SE, s_stats)], incs=[(VE, 1)])
                sl = 2 if is0 else 1
                E.op("vector", lambda e, a=a, sl=sl: e.tensor_mul(
                    rs_t[:, :], rs_t[:, :], reps[:, sl, a:a + 1024]),
                     incs=[(VE, 1)])
                if is0:
                    E.op("vector", lambda e, c=c, a=a, b=b: e.tensor_scalar(
                        xn_t[:, c, a + 1:b + 1], rs_t[:, :], gb0_t[:, c:c + 1],
                        gb0_t[:, 8 + c:9 + c], ALU.mult, ALU.add),
                         incs=[(VE, 1)])
                else:
                    E.op("vector", lambda e, c=c, a=a, b=b: e.tensor_copy(
                        xn_t[:, c, a + 1:b + 1], rs_t[:, :]),
                         incs=[(VE, 1)])
        v_xn = VE.val
        sd_tab = 0
        if is0:
            E.op("sync", lambda e: e.dma_start(ctab_t[:, :], ctab_d),
                 waits=[(PP, PP.val), (PP2, PP2.val), (VE, VE.val)],
                 incs=[(SD, 16)])
            E.op("sync", lambda e: e.dma_start(stab_view[:, :], stab_d),
                 waits=[(VE, v_xn)], incs=[(SD, 16)])
            sd_tab = SD.val
        E.op("vector", lambda e: e.memset(vt_t[:, :, :, 64:128], 0.0),
             incs=[(VE, 1)])
        E.op("vector", lambda e: e.memset(vt_t[:, :, :, 64:65], 1.0),
             incs=[(VE, 1)])
        v_vt_init = VE.val

        wsrc = wqkv0 if is0 else wqkv1
        bsrc = bqkv0_t if is0 else bqkv1_t

        gd_pair = [None] * NPAIR
        te_entry = TE.val
        E.op("gpsimd", lambda e: e.dma_start(
            wqkv_t[:, :, :], wsrc[0].rearrange("k p c -> p k c")),
             waits=[(TE, te_entry)], incs=[(WQ, 16)])
        gd_pair[0] = (WQ, WQ.val)

        v_head_done = v_vt_init
        se_head_done = SE.val
        exp_guard = [0, 0]
        ctx_read_ve = [v_vt_init, v_vt_init]
        bc_read_ve = [0]
        hq_parity = [0]
        cstg_sd = 0
        te_qkv_prev = te_entry

        for p in range(NPAIR):
            def mk_mm(colbase):
                def f(t, colbase=colbase):
                    r = None
                    for g in range(4):
                        for kc in range(FC):
                            r = t.matmul(pa[:, g * 512:(g + 1) * 512],
                                         wqkv_t[:, kc, colbase:colbase + 128],
                                         xn_t[:, kc, 1 + g * 512:1 + (g + 1) * 512],
                                         start=(kc == 0), stop=(kc == FC - 1))
                    return r
                return f
            E.op("tensor", mk_mm(0),
                 waits=[gd_pair[p], (VE, max(v_xn, v_head_done)),
                        (SE, se_head_done)],
                 incs=[(TE, 1)])
            te_q = TE.val
            E.op("vector", lambda e, p=p: e.tensor_scalar(
                qb_t[:, :], pa[:, :], bsrc[:, 0, p:p + 1], 0.125, ALU.add,
                ALU.mult),
                 waits=[(TE, te_q)] + ([(SD, sd_tab)] if is0 else []),
                 incs=[(VE, 1)])
            v_q = VE.val
            if is0:
                def rotq(t):
                    r = None
                    for g in range(4):
                        r = t.matmul(pb[:, g * 512:(g + 1) * 512], rmat_b[:, :],
                                     qb_t[:, g * 512:(g + 1) * 512],
                                     start=True, stop=True)
                    return r
                E.op("tensor", rotq, waits=[(VE, v_q)], incs=[(TE, 1)])
                te_rq = TE.val
                E.op("vector", lambda e: e.tensor_mul(pb[:, :], pb[:, :],
                                                      stab_view[:, :]),
                     waits=[(TE, te_rq)], incs=[(VE, 1)])
                E.op("vector", lambda e: e.tensor_mul(qb_t[:, :], qb_t[:, :],
                                                      ctab_t[:, :]),
                     incs=[(VE, 1)])
                E.op("vector", lambda e: e.tensor_add(qb_t[:, :], qb_t[:, :],
                                                      pb[:, :]),
                     incs=[(VE, 1)])
            v_q_done = VE.val
            E.op("tensor", mk_mm(128), waits=[(VE, v_q)], incs=[(TE, 1)])
            te_k = TE.val
            E.op("vector", lambda e, p=p: e.tensor_scalar(
                kb_t[:, :], pa[:, :], bsrc[:, 1, p:p + 1], None, ALU.add),
                 waits=[(TE, te_k)], incs=[(VE, 1)])
            v_k = VE.val
            if is0:
                def rotk(t):
                    r = None
                    for g in range(4):
                        r = t.matmul(pb[:, g * 512:(g + 1) * 512], rmat_b[:, :],
                                     kb_t[:, g * 512:(g + 1) * 512],
                                     start=True, stop=True)
                    return r
                E.op("tensor", rotk, waits=[(VE, v_k)], incs=[(TE, 1)])
                te_rk = TE.val
                E.op("vector", lambda e: e.tensor_mul(pb[:, :], pb[:, :],
                                                      stab_view[:, :]),
                     waits=[(TE, te_rk)], incs=[(VE, 1)])
                E.op("vector", lambda e: e.tensor_mul(kb_t[:, :], kb_t[:, :],
                                                      ctab_t[:, :]),
                     incs=[(VE, 1)])
                E.op("vector", lambda e: e.tensor_add(kb_t[:, :], kb_t[:, :],
                                                      pb[:, :]),
                     incs=[(VE, 1)])
            v_k_done = VE.val
            E.op("tensor", mk_mm(256), waits=[(VE, v_k)], incs=[(TE, 1)])
            te_v = TE.val
            te_qkv_prev = TE.val
            if p + 1 < NPAIR:
                E.op("gpsimd", lambda e, p=p: e.dma_start(
                    wqkv_t[:, :, :], wsrc[p + 1].rearrange("k p c -> p k c")),
                     waits=[(TE, te_qkv_prev)], incs=[(WQ, 16)])
                gd_pair[p + 1] = (WQ, WQ.val)
            for half in range(2):
                ta2 = half * 1024
                E.op("vector", lambda e, p=p, ta2=ta2: e.tensor_scalar(
                    vstg_t[:, :], pa[:, ta2:ta2 + 1024], bsrc[:, 2, p:p + 1],
                    None, ALU.add),
                     waits=[(TE, te_v)], incs=[(VE, 1)])
                v_vs = VE.val

                def vtrans(t, half=half):
                    r = None
                    for hh in range(2):
                        b0 = hh * 64
                        for kc in range(8):
                            r = t.transpose(
                                pb[:, hh * 512 + kc * 64:hh * 512 + kc * 64 + 64],
                                vstg_t[b0:b0 + 64, kc * 128:(kc + 1) * 128],
                                ident_f[b0:b0 + 64, b0:b0 + 64],
                                tile_position=(b0, 0))
                    return r
                E.op("tensor", vtrans, waits=[(VE, v_vs)], incs=[(TE, 1)])
                te_vt = TE.val
                for hh in range(2):
                    E.op("vector", lambda e, hh=hh, half=half: e.tensor_copy(
                        vt_t[:, hh, half * 8:half * 8 + 8, 0:64],
                        pb[:, hh * 512:(hh + 1) * 512]),
                         waits=[(TE, te_vt)], incs=[(VE, 1)])
            v_vt = VE.val
            te_n_prev = 0
            for src_t, outc, vguard in ((qb_t, 0, v_q_done), (kb_t, 1, v_k_done)):
                for hf in range(2):
                    a = hf * 1024
                    E.op("scalar", lambda e, src_t=src_t, a=a: e.activation(
                        qsq_t[:, :], src_t[:, a:a + 1024], AF.Square),
                         waits=[(VE, max(vguard, v_vt)), (TE, te_n_prev)],
                         incs=[(SE, 1)])
                    s_qs = SE.val

                    def mmn(t):
                        r = None
                        for g in range(2):
                            r = t.matmul(pb[:, 1024 + g * 512:1024 + (g + 1) * 512],
                                         ones_b[:, :],
                                         qsq_t[:, g * 512:(g + 1) * 512],
                                         start=True, stop=True)
                        return r
                    E.op("tensor", mmn,
                         waits=[(SE, s_qs), (VE, max(v_vt, VE.val))],
                         incs=[(TE, 1)])
                    te_n = TE.val
                    te_n_prev = te_n
                    if hf == 0:
                        E.op("vector", lambda e, outc=outc: e.reduce_max(
                            nmx_t[:, outc:outc + 1], pb[:, 1024:2048], axis=AX),
                             waits=[(TE, te_n)], incs=[(VE, 1)])
                    else:
                        E.op("vector", lambda e: e.reduce_max(
                            nmx_t[:, 2:3], pb[:, 1024:2048], axis=AX),
                             waits=[(TE, te_n)], incs=[(VE, 1)])
                        E.op("vector", lambda e, outc=outc: e.tensor_max(
                            nmx_t[:, outc:outc + 1], nmx_t[:, outc:outc + 1],
                            nmx_t[:, 2:3]),
                             incs=[(VE, 1)])
            E.op("vector", lambda e: e.tensor_mul(nmx_t[:, 2:3], nmx_t[:, 0:1],
                                                  nmx_t[:, 1:2]),
                 incs=[(VE, 1)])
            v_nm = VE.val
            E.op("scalar", lambda e: e.activation(nmx_t[:, 2:3], nmx_t[:, 2:3],
                                                  AF.Sqrt),
                 waits=[(VE, v_nm)], incs=[(SE, 1)])
            s_nm = SE.val
            E.op("vector", lambda e: e.tensor_scalar(nmx_t[:, 3:4], nmx_t[:, 2:3],
                                                     -1.0, None, ALU.mult),
                 waits=[(SE, s_nm)], incs=[(VE, 1)])
            v_negm = VE.val

            for hh in range(2):
                b0 = hh * 64
                tp = (64, 0) if b0 else None
                for qh in range(2):
                    qa = qh * 1024
                    par = hq_parity[0]
                    hq_parity[0] ^= 1
                    cbank = par * 1024
                    te_sc = [0] * 16
                    te_cx = [0] * 16
                    se_ex = [0] * 16

                    def emit_cx(kc, hh=hh, cbank=cbank, par=par):
                        def mmcx(t, kc=kc, hh=hh, cbank=cbank):
                            r = None
                            for g in range(2):
                                r = t.matmul(
                                    pb[:, cbank + g * 512:cbank + (g + 1) * 512],
                                    vt_t[:, hh, kc, :],
                                    pt_t[:, kc % 2, g * 512:(g + 1) * 512],
                                    start=(kc == 0), stop=(kc == 15))
                            return r
                        cw = [(SE, se_ex[kc])]
                        if kc == 0:
                            cw.append((VE, ctx_read_ve[par]))
                        E.op("tensor", mmcx, waits=cw, incs=[(TE, 1)])
                        te_cx[kc] = TE.val

                    for kc in range(16):
                        sbank = (kc % 2) * 1024
                        wts = [(VE, v_negm),
                               (SE, se_ex[kc - 2] if kc >= 2 else exp_guard[kc % 2])]
                        if kc == 1:
                            wts.append((VE, bc_read_ve[0]))

                        def mmsc(t, kc=kc, sbank=sbank, b0=b0, qa=qa, tp=tp):
                            r = None
                            for g in range(2):
                                r = t.matmul(
                                    pa[:, sbank + g * 512:sbank + (g + 1) * 512],
                                    kb_t[b0:b0 + 64, kc * 128:(kc + 1) * 128],
                                    qb_t[b0:b0 + 64, qa + g * 512:qa + (g + 1) * 512],
                                    start=True, stop=True, tile_position=tp)
                            return r
                        E.op("tensor", mmsc, waits=wts, incs=[(TE, 1)])
                        te_sc[kc] = TE.val
                        wts = [(TE, te_sc[kc])]
                        if kc >= 2:
                            wts.append((TE, te_cx[kc - 2]))
                        E.op("scalar", lambda e, kc=kc, sbank=sbank: e.activation(
                            pt_t[:, kc % 2, :], pa[:, sbank:sbank + 1024], AF.Exp,
                            bias=nmx_t[:, 3:4]),
                             waits=wts, incs=[(SE, 1)])
                        se_ex[kc] = SE.val
                        if kc >= 1:
                            emit_cx(kc - 1)
                    emit_cx(15)
                    exp_guard[0] = se_ex[14]
                    exp_guard[1] = se_ex[15]
                    E.op("scalar", lambda e, cbank=cbank: e.activation(
                        rs_t[64:65, :], pb[64:65, cbank:cbank + 1024], AF.Ln),
                         waits=[(TE, te_cx[15])], incs=[(SE, 1)])
                    E.op("scalar", lambda e: e.activation(
                        rinv_t[64:65, :], rs_t[64:65, :], AF.Exp, scale=-1.0),
                         incs=[(SE, 1)])
                    s_ri = SE.val

                    def mmbc(t):
                        r = None
                        for g in range(2):
                            r = t.matmul(
                                pa[0:64, 1024 + g * 512:1024 + (g + 1) * 512],
                                ones_b[64:65, 0:64],
                                rinv_t[64:65, g * 512:(g + 1) * 512],
                                start=True, stop=True, tile_position=(64, 0))
                        return r
                    E.op("tensor", mmbc,
                         waits=[(SE, s_ri), (VE, bc_read_ve[0])], incs=[(TE, 1)])
                    te_bc = TE.val
                    E.op("vector", lambda e: e.tensor_copy(
                        rs_t[0:64, :], pa[0:64, 1024:2048]),
                         waits=[(TE, te_bc)], incs=[(VE, 1)])
                    bc_read_ve[0] = VE.val
                    h = 2 * p + hh
                    cchunk, crow = h // 2, (h % 2) * 64
                    if crow == 0:
                        E.op("vector", lambda e, cchunk=cchunk, qa=qa, cbank=cbank:
                             e.tensor_mul(
                                 ctx_all[0:64, cchunk, qa:qa + 1024],
                                 pb[0:64, cbank:cbank + 1024], rs_t[0:64, :]),
                             incs=[(VE, 1)])
                    else:
                        E.op("vector", lambda e, cbank=cbank: e.tensor_mul(
                            cstg_t[:, :], pb[0:64, cbank:cbank + 1024],
                            rs_t[0:64, :]),
                             waits=[(SD, cstg_sd)], incs=[(VE, 1)])
                        v_cs = VE.val
                        E.op("sync", lambda e, cchunk=cchunk, qa=qa: e.dma_start(
                            ctx_all[64:128, cchunk, qa:qa + 1024], cstg_t[:, :]),
                             waits=[(VE, v_cs)], incs=[(SD, 16)])
                        cstg_sd = SD.val
                    ctx_read_ve[par] = VE.val
                    v_head_done = VE.val
            se_head_done = SE.val

        sd_ctx = SD.val
        v_ctx = VE.val
        se_now = SE.val
        wdst = wout0 if is0 else wo1
        bvec = bout0_t if is0 else bo1_t
        gd_oc = [None] * FC
        for s in range(2):
            E.op("gpsimd", lambda e, s=s: e.dma_start(
                wbufA[:, s, :, :], wdst[s].rearrange("k p c -> p k c")),
                 waits=[(TE, te_qkv_prev)], incs=[(WA[s], 16)])
            gd_oc[s] = (WA[s], WA[s].val)
        wo_last_te = [0, 0]
        v_ep = VE.val
        for oc in range(FC):
            slot = oc % 2
            for hf in range(2):
                a = hf * 1024

                def mmo(t, slot=slot, a=a):
                    r = None
                    for g in range(2):
                        for kc in range(FC):
                            r = t.matmul(pa[:, a + g * 512:a + (g + 1) * 512],
                                         wbufA[:, slot, kc, :],
                                         ctx_all[:, kc, a + g * 512:a + (g + 1) * 512],
                                         start=(kc == 0), stop=(kc == FC - 1))
                    return r
                E.op("tensor", mmo,
                     waits=[gd_oc[oc], (SD, sd_ctx),
                            (VE, max(v_ctx, v_ep)), (SE, se_now)],
                     incs=[(TE, 1)])
                te_o = TE.val
                if is0:
                    E.op("vector", lambda e, oc=oc, a=a: e.tensor_sub(
                        rs_t[:, :], x_t[:, oc, a:a + 1024], reps[:, 0, a:a + 1024]),
                         waits=[(TE, te_o)], incs=[(VE, 1)])
                    E.op("vector", lambda e, a=a: e.tensor_mul(
                        rs_t[:, :], rs_t[:, :], reps[:, 1, a:a + 1024]),
                         incs=[(VE, 1)])
                    E.op("vector", lambda e, oc=oc, a=a: e.scalar_tensor_tensor(
                        rs_t[:, :], pa[:, a:a + 1024], bvec[:, oc:oc + 1],
                        rs_t[:, :], ALU.add, ALU.add),
                         incs=[(VE, 1)])
                    E.op("vector", lambda e, oc=oc, a=a: e.tensor_add(
                        x_t[:, oc, a:a + 1024], x_t[:, oc, a:a + 1024], rs_t[:, :]),
                         incs=[(VE, 1)])
                else:
                    E.op("vector", lambda e, oc=oc, a=a: e.scalar_tensor_tensor(
                        x_t[:, oc, a:a + 1024], pa[:, a:a + 1024],
                        bvec[:, oc:oc + 1], x_t[:, oc, a:a + 1024],
                        ALU.add, ALU.add),
                         waits=[(TE, te_o)], incs=[(VE, 1)])
            v_ep = VE.val
            wo_last_te[slot] = TE.val
            if oc + 2 < FC:
                E.op("gpsimd", lambda e, oc=oc, slot=slot: e.dma_start(
                    wbufA[:, slot, :, :], wdst[oc + 2].rearrange("k p c -> p k c")),
                     waits=[(TE, wo_last_te[slot])], incs=[(WA[slot], 16)])
                gd_oc[oc + 2] = (WA[slot], WA[slot].val)


    w1_read_guard = [0]
    w2_read_guard = [0]

    def emit_ffn1():
        s_stats = emit_ln(lnxc_t, lnxs_t)
        for c in range(FC):
            for hf in range(2):
                a, b = hf * 1024, (hf + 1) * 1024
                E.op("vector", lambda e, c=c, a=a, b=b: e.tensor_sub(
                    rs_t[:, :], x_t[:, c, a:b], reps[:, 0, a:b]),
                     waits=[(SE, s_stats)], incs=[(VE, 1)])
                E.op("vector", lambda e, c=c, a=a, b=b: e.tensor_mul(
                    xnf_t[:, c, a:b], rs_t[:, :], reps[:, 1, a:b]),
                     incs=[(VE, 1)])
        v_xn = VE.val

        for tt in range(2):
            ta = tt * 1024
            gd_w1 = [None] * 32
            w1_last_te = [0, 0]
            te_entry = TE.val
            for s in range(2):
                E.op("gpsimd", lambda e, s=s: e.dma_start(
                    wbufA[:, s, :, :],
                    wq1s[s * 8:(s + 1) * 8, :, :].rearrange("k p c -> p k c")),
                     waits=list(pp_spill_guard[0]) +
                           [(TE, max(w1_read_guard[0], te_entry))],
                     incs=[(WA[s], 16)])
                gd_w1[s] = (WA[s], WA[s].val)
            se_h = SE.val
            ve_h = VE.val
            for oc in range(32):
                slot = oc % 2
                ph = pa if oc % 2 == 0 else pb

                def mmh(t, slot=slot, ph=ph, ta=ta):
                    r = None
                    for g in range(2):
                        for kc in range(FC):
                            r = t.matmul(ph[:, g * 512:(g + 1) * 512],
                                         wbufA[:, slot, kc, :],
                                         xnf_t[:, kc, ta + g * 512:ta + (g + 1) * 512],
                                         start=(kc == 0), stop=(kc == FC - 1))
                    return r
                E.op("tensor", mmh,
                     waits=[gd_w1[oc], (VE, max(v_xn, ve_h)), (SE, se_h)],
                     incs=[(TE, 1)])
                w1_last_te[slot] = TE.val
                te_h = TE.val
                dst = hlo_t if oc < 16 else hhi_t
                if oc % 2 == 0:
                    E.op("vector", lambda e, dst=dst, oc=oc, ph=ph: e.tensor_scalar(
                        dst[:, oc % 16, :], ph[:, 0:1024], b1_t[:, oc:oc + 1],
                        0.0, ALU.add, ALU.max),
                         waits=[(TE, te_h)], incs=[(VE, 1)])
                    ve_h = VE.val
                else:
                    E.op("scalar", lambda e, dst=dst, oc=oc, ph=ph: e.activation(
                        dst[:, oc % 16, :], ph[:, 0:1024], AF.Relu,
                        bias=b1_t[:, oc:oc + 1]),
                         waits=[(TE, te_h)], incs=[(SE, 1)])
                    se_h = SE.val
                if oc + 2 < 32:
                    E.op("gpsimd", lambda e, oc=oc, slot=slot: e.dma_start(
                        wbufA[:, slot, :, :],
                        wq1s[(oc + 2) * 8:(oc + 3) * 8, :, :]
                        .rearrange("k p c -> p k c")),
                         waits=[(TE, w1_last_te[slot])], incs=[(WA[slot], 16)])
                    gd_w1[oc + 2] = (WA[slot], WA[slot].val)
            w1_read_guard[0] = TE.val
            v_hdone = VE.val
            s_hdone = SE.val

            gd_w2 = [None] * (FC * 2)
            w2_last_te = [TE.val]
            for s in range(2):
                E.op("gpsimd", lambda e, s=s: e.dma_start(
                    w2buf[:, s, :, :],
                    wq2s[s * 16:(s + 1) * 16, :, :].rearrange("k p c -> p k c")),
                     waits=list(pp_spill_guard[0]) +
                           [(PP, PP.val), (PP2, PP2.val),
                            (TE, max(w2_read_guard[0], w1_read_guard[0])),
                            (VE, pp_w2_ve_guard[0])],
                     incs=[(W2S[s], 16)])
                gd_w2[s] = (W2S[s], W2S[s].val)
            v_ep2 = VE.val
            for mc in range(FC):
                ph = pa if mc % 2 == 0 else pb

                def mmo2(t, mc=mc, ph=ph):
                    r = None
                    for g in range(2):
                        for kc in range(32):
                            half, idx = kc // 16, kc % 16
                            r = t.matmul(ph[:, g * 512:(g + 1) * 512],
                                         w2buf[:, half, idx, :],
                                         (hlo_t if kc < 16 else hhi_t)
                                         [:, kc % 16, g * 512:(g + 1) * 512],
                                         start=(kc == 0), stop=(kc == 31))
                    return r
                E.op("tensor", mmo2,
                     waits=[gd_w2[2 * mc], gd_w2[2 * mc + 1],
                            (VE, max(v_hdone, v_ep2)), (SE, s_hdone)],
                     incs=[(TE, 1)])
                w2_last_te[0] = TE.val
                te_o2 = TE.val
                E.op("vector", lambda e, mc=mc, ph=ph: e.tensor_scalar(
                    rs_t[:, :], ph[:, 0:1024], b2_t[:, mc:mc + 1], 0.5,
                    ALU.add, ALU.mult),
                     waits=[(TE, te_o2)], incs=[(VE, 1)])
                E.op("vector", lambda e, mc=mc, ta=ta: e.tensor_add(
                    x_t[:, mc, ta:ta + 1024], x_t[:, mc, ta:ta + 1024],
                    rs_t[:, :]),
                     incs=[(VE, 1)])
                v_ep2 = VE.val
                if mc + 1 < FC:
                    for s in range(2):
                        E.op("gpsimd", lambda e, mc=mc, s=s: e.dma_start(
                            w2buf[:, s, :, :],
                            wq2s[(mc + 1) * 32 + s * 16:
                                 (mc + 1) * 32 + (s + 1) * 16, :, :]
                            .rearrange("k p c -> p k c")),
                             waits=[(TE, w2_last_te[0])], incs=[(W2S[s], 16)])
                        gd_w2[2 * (mc + 1) + s] = (W2S[s], W2S[s].val)
            w2_read_guard[0] = TE.val
            pp_w2_ve_guard[0] = 0

    pp_w2_ve_guard = [0]


    seq = []
    if n_sub >= 1:
        seq.append("ffn0a")
    if n_sub >= 2:
        seq.append("attn0")
    if n_sub >= 3:
        seq.append("ffn0b")
    if n_sub >= 4:
        seq.append("ffn1a")
    if n_sub >= 5:
        seq.append("attn1")
    if n_sub >= 6:
        seq.append("ffn1b")

    need_pp = n_sub >= 4
    for ph in seq:
        if ph == "ffn0a":
            if need_pp:
                queue_prepass_gamma()
            emit_ffn0()
            if need_pp:
                pp_flush()
                emit_prepass_gamma_fin()
        elif ph == "attn0":
            emit_attn(0)
        elif ph == "ffn0b":
            if need_pp:
                queue_prepass_quant()
            emit_ffn0()
            if need_pp:
                pp_flush()
        elif ph == "ffn1a":
            pp_w2_ve_guard[0] = E.sems["VE"].val
            emit_ffn1()
        elif ph == "attn1":
            emit_attn(1)
        elif ph == "ffn1b":
            emit_ffn1()
        E.barrier()

    v_fin, s_fin, t_fin = VE.val, SE.val, TE.val
    E.op("sync", lambda e: e.dma_start(xout.rearrange("c p t -> p c t"),
                                       x_t[:, :, :]),
         waits=[(VE, v_fin), (SE, s_fin), (TE, t_fin)], incs=[(SD, 16)])
    sd_out = SD.val
    E.op("sync", lambda e: None, waits=[(SD, sd_out)])
    E.op("gpsimd", lambda e: None, waits=[(SD, sd_out)])

    import contextlib
    with contextlib.ExitStack() as stack:
        for n in list(E.sems):
            E.sems[n].h = stack.enter_context(nc.semaphore(n.lower()))
        block = stack.enter_context(nc.Block())

        @block.tensor
        def _(t):
            for f in E.stream["tensor"]:
                f(t)

        @block.vector
        def _(v):
            for f in E.stream["vector"]:
                f(v)

        @block.scalar
        def _(s):
            for f in E.stream["scalar"]:
                f(s)

        @block.sync
        def _(s):
            for f in E.stream["sync"]:
                f(s)

        @block.gpsimd
        def _(g):
            for f in E.stream["gpsimd"]:
                f(g)

    return nc




def kernel(**inputs):
    d, xcores = _prep(inputs)
    key = N_SUB
    if key not in _CACHE:
        _CACHE[key] = build_nc(N_SUB)
    nc = _CACHE[key]
    in_maps = []
    for i in range(8):
        m = dict(d)
        m["xin"] = xcores[i]
        in_maps.append(m)
    res = run_bass_kernel_spmd(nc, in_maps, core_ids=list(range(8)))
    return np.stack([res.results[i]["xout"].reshape(C, T).T for i in range(8)],
                    axis=0)
